# version 25
# baseline (speedup 1.0000x reference)
import numpy as np
import ml_dtypes
from contextlib import ExitStack
import concourse.bass as bass
import concourse.mybir as mybir
from concourse.bass_utils import run_bass_kernel_spmd


F32 = mybir.dt.float32
BF16 = mybir.dt.bfloat16
F32R = mybir.dt.float32r
AF = mybir.ActivationFunctionType
ALU = mybir.AluOpType

NCORES = 8
T = 1024
TH = T + 3
D = 2048
INTER = 4096
HEADS = 64
P = 64
N = 128
CONV = INTER + 2 * N
NF_XBC = CONV // 128
NF_G = INTER // 128
ND = D // 128
Q = 128
NCH = T // Q
FFI = 8192
EPS = 1e-5
RC = float(np.float32(1.5 * 2 ** 23))
SHARED_AG = True

TCH = [(0, 512), (512, 512), (1024, 3)]
TCO = [(0, 512), (512, 512)]


class Sem:
    def __init__(self, nc, es, name):
        self.h = es.enter_context(nc.semaphore(name))
        self.n = 0
    def inc(self, k=1):
        self.n += k
        return self.n


class Builder:
    def __init__(self, debug=()):
        self.debug = set(debug)
        self.nc = bass.Bass("TRN2", target_bir_lowering=False)
        self.es = ExitStack()
        self.ops = {k: [] for k in ("tensor", "vector", "scalar", "gpsimd", "sync")}
        self._consts = {}
        self._nbar = 0

    def emit(self, eng, fn):
        self.ops[eng].append(fn)

    def barrier(self):
        s = Sem(self.nc, self.es, f"bar{self._nbar}")
        self._nbar += 1
        engs = ("tensor", "vector", "scalar", "gpsimd", "sync")
        for e in engs:
            s.inc(1)
            self.emit(e, lambda x, h=s.h: x.drain().then_inc(h, 1))
        tot = s.n
        for e in engs:
            self.emit(e, lambda x, h=s.h, tot=tot: x.wait_ge(h, tot))

    def wait(self, eng, sem, val=None):
        v = sem.n if val is None else val
        assert v <= sem.n, "waiting on a future semaphore value"
        self.emit(eng, lambda e, h=sem.h, v=v: e.wait_ge(h, v))

    def dma(self, eng, sem, out, in_, **kw):
        v = sem.inc(16)
        self.emit(eng, lambda e, o=out, i=in_, h=sem.h, kw=kw: e.dma_start(out=o, in_=i, **kw).then_inc(h, 16))
        return v

    def sb(self, name, shape, dtype, off):
        return self.nc.alloc_sbuf_tensor_at(name, list(shape), dtype, offset=off)

    def const(self, name, arr):
        if name not in self._consts:
            self._consts[name] = self.nc.inline_tensor(np.ascontiguousarray(arr), name)
        return self._consts[name]

    def dbg(self, name, shape, dtype=F32):
        return self.nc.dram_tensor("dbg_" + name, list(shape), dtype, kind="ExternalOutput").ap()

    def finish(self):
        nc = self.nc
        with nc.Block() as block:
            @block.tensor
            def _(e):
                for fn in self.ops["tensor"]: fn(e)
            @block.vector
            def _(e):
                for fn in self.ops["vector"]: fn(e)
            @block.scalar
            def _(e):
                for fn in self.ops["scalar"]: fn(e)
            @block.gpsimd
            def _(e):
                for fn in self.ops["gpsimd"]: fn(e)
            @block.sync
            def _(e):
                for fn in self.ops["sync"]: fn(e)
        self.es.close()
        return nc


def build(debug=()):
    b = Builder(debug)
    nc = b.nc
    es = b.es

    dram_in = lambda n, s: nc.dram_tensor(n, list(s), F32, kind="ExternalInput").ap()
    x_in = dram_in("x_in", [D, TH])
    w_xbc = dram_in("w_xbc", [D, CONV])
    w_z = dram_in("w_z", [D, INTER])
    w_dt = dram_in("w_dt", [D, HEADS])
    w_out = dram_in("w_out", [INTER, D])
    cw_in = dram_in("cw_in", [128, NF_XBC * 4])
    cb_in = dram_in("cb_in", [128, NF_XBC])
    dtb_in = dram_in("dtb_in", [64, 1])
    alog_in = dram_in("alog_in", [64, 1])
    dvec_in = dram_in("dvec_in", [64, 1])
    nw_in = dram_in("nw_in", [128, NF_G])
    wg_sh = dram_in("wg_sh", [D, T])
    wu_sh = dram_in("wu_sh", [D, T])
    wd_sh = dram_in("wd_sh", [FFI, 256])
    smask = dram_in("smask", [128, 1])
    y_out = nc.dram_tensor("y_out", [D, T], F32, kind="ExternalOutput").ap()
    x1_dram = nc.dram_tensor("x1_scratch", [D, T], F32, kind="Internal").ap()
    gate_dram = nc.dram_tensor("gate_scratch", [INTER, T], BF16, kind="Internal").ap()
    zdram = nc.dram_tensor("z_scratch", [FFI, T], BF16, kind="Internal").ap()
    wqg_in = nc.dram_tensor("wqg_in", [D, T], BF16, kind="Internal")
    wqu_in = nc.dram_tensor("wqu_in", [D, T], BF16, kind="Internal")
    wqd_in = nc.dram_tensor("wqd_in", [FFI, 256], BF16, kind="Internal")
    asp = "Shared" if SHARED_AG else "Local"
    wqg_out = nc.dram_tensor("wqg_out", [8 * D, T], BF16, kind="Internal", addr_space=asp)
    wqu_out = nc.dram_tensor("wqu_out", [8 * D, T], BF16, kind="Internal", addr_space=asp)
    wqd_out = nc.dram_tensor("wqd_out", [8 * FFI, 256], BF16, kind="Internal", addr_space=asp)
    ar_in = nc.dram_tensor("ar_in", [1, 4], F32, kind="Internal")
    ar_out = nc.dram_tensor("ar_out", [1, 4], F32, kind="Internal")

    s_ld = Sem(nc, es, "ld")
    s_wp = Sem(nc, es, "wp")
    s_t = Sem(nc, es, "t")
    s_v = Sem(nc, es, "v")
    s_s = Sem(nc, es, "s")
    s_st = Sem(nc, es, "st")
    s_gst = Sem(nc, es, "gst")

    SB0 = 16512
    A0, B0, C0, D0, E0 = SB0, SB0 + 65536, SB0 + 131072, SB0 + 164352, SB0 + 180224
    regAB_h = b.sb("regAB", [128, 65536], BF16, A0)
    regC_h = b.sb("regC", [128, 16640], BF16, C0)
    regE_h = b.sb("regE", [128, 10320], BF16, E0)
    regAB = regAB_h.ap(); regABf = regAB.bitcast(F32)
    regC = regC_h.ap(); regCf = regC.bitcast(F32)
    regE = regE_h.ap(); regEf = regE.bitcast(F32)
    xtmp = [regABf[:, i * 1032: i * 1032 + TH] for i in range(ND)]
    xpart = regAB[:, 0:NCH * NF_G * Q]
    lnx1 = [regC[:, i * 1040: i * 1040 + TH] for i in range(ND)]
    Bc = b.sb("Bc", [128, T], BF16, D0)
    Cc = b.sb("Cc", [128, T], BF16, D0 + 2048)
    dt32 = b.sb("dt32", [64, TH], F32, D0 + 4096)
    cwt = b.sb("cwt", [128, NF_XBC * 4], F32, D0 + 8224)
    cbt = b.sb("cbt", [128, NF_XBC], F32, D0 + 8768)
    abt = b.sb("abt", [64, 1], F32, D0 + 8928)
    dbt = b.sb("dbt", [64, 1], F32, D0 + 8960)
    dtbt = b.sb("dtbt", [64, 1], F32, D0 + 8992)
    abtA = b.sb("abtA", [64, 1], F32, D0 + 9024)
    mbc = regEf[:, 0:TH]
    rbc = regEf[:, 1032:1032 + TH]
    cv0 = regEf[:, 2064:2064 + TH]
    cv1 = regEf[:, 3096:3096 + T]
    xbf = [regE[:, 8240 + j * 1040: 8240 + j * 1040 + TH] for j in range(2)]
    wpan = [b.sb(f"wpan{j}", [128, 2048], BF16, E0 + 20640 + j * 4096) for j in range(2)]
    ones_sb = b.sb("ones_sb", [128, 128], BF16, E0 + 28832)
    ctile = b.sb("ctile", [128, 8], F32, E0 + 29088)
    identf = b.sb("identf", [128, 128], F32, E0 + 29152)
    sel127 = b.sb("sel127", [128, 128], F32, E0 + 29664)
    triu_m = b.sb("triu_m", [128, Q], F32, E0 + 30176)
    ones1f = b.sb("ones1f", [33, 128], F32, E0 + 30688)
    identb = b.sb("identb", [128, 128], BF16, E0 + 31200)
    Dbc = b.sb("Dbc", [128, 64], F32, E0 + 31456)
    dvr = b.sb("dvr", [1, 64], F32, E0 + 31712)
    smt = b.sb("smt", [128, 1], F32, E0 + 31968)
    onescol = b.sb("onescol", [128, 1], F32, E0 + 32000)
    H128_sb = b.sb("H128_sb", [128, 128], BF16, E0 + 32032)
    H64b_sb = b.sb("H64b_sb", [128, 128], BF16, E0 + 32288)
    nwt = b.sb("nwt", [128, NF_G], F32, D0 + 9056)
    mrow = b.sb("mrow", [1, 16], F32, D0 + 9184)
    wqrow = b.sb("wqrow", [128, 8], F32, D0 + 9248)
    amtm = b.sb("amtm", [128, 8], F32, D0 + 9280)
    coltl = b.sb("coltl", [128, 128], F32, D0 + 9312)

    CVALS = [float(EPS), 1.0, -0.5, RC, -RC, 10.0, 2.0, 0.5]
    for ci, v in enumerate(CVALS):
        nc.const_aps.register(F32, v, ctile[:, ci:ci + 1])
    cvals_c = b.const("cvals_c", np.tile(np.array(CVALS, np.float32), (128, 1)))

    pb = [es.enter_context(nc.psum_tensor(f"pb{i}", [128, 512], F32)) for i in range(8)]

    ones_c = b.const("ones_c", np.ones((128, 128), dtype=ml_dtypes.bfloat16))
    def _had(n):
        h = np.array([[1.0]], np.float32)
        while h.shape[0] < n:
            h = np.block([[h, h], [h, -h]])
        return h
    H128_c = b.const("H128_c", _had(128).astype(ml_dtypes.bfloat16))
    _h64 = _had(64)
    _hb = np.zeros((128, 128), np.float32)
    _hb[0:64, 0:64] = _h64
    _hb[64:128, 64:128] = _h64
    H64b_c = b.const("H64b_c", _hb.astype(ml_dtypes.bfloat16))


    b.dma("sync", s_ld, ctile[:], cvals_c.ap())
    b.dma("sync", s_ld, ones_sb[:], ones_c.ap())
    onescol_c = b.const("onescol_c", np.ones((128, 1), np.float32))
    b.dma("sync", s_ld, onescol[:], onescol_c.ap())
    identf_c = b.const("identf_c", np.eye(128, dtype=np.float32))
    identb_c = b.const("identb_c", np.eye(128).astype(ml_dtypes.bfloat16))
    _o1 = np.zeros((33, 128), np.float32); _o1[0] = 1.0; _o1[32] = 1.0
    ones1f_c = b.const("ones1f_c", _o1)
    s127_c = b.const("s127_c", (np.arange(128)[:, None] == 127).astype(np.float32) * np.ones((1, 128), np.float32))
    triu_c = b.const("triu_c", np.triu(np.ones((128, 128), np.float32)))
    b.dma("sync", s_ld, identf[:], identf_c.ap())
    b.dma("sync", s_ld, identb[:], identb_c.ap())
    b.dma("sync", s_ld, ones1f[:], ones1f_c.ap())
    b.dma("sync", s_ld, sel127[:], s127_c.ap())
    b.dma("sync", s_ld, triu_m[:], triu_c.ap())
    b.dma("sync", s_ld, dvr[:], dvec_in.transpose([1, 0]))
    b.dma("sync", s_ld, cwt[:], cw_in)
    b.dma("sync", s_ld, cbt[:], cb_in)
    b.dma("sync", s_ld, abt[:], alog_in)
    b.dma("sync", s_ld, dbt[:], dvec_in)
    b.dma("sync", s_ld, dtbt[:], dtb_in)
    ld_small = s_ld.n

    for i in range(ND):
        b.dma("sync", s_ld, xtmp[i][:], x_in[i * 128:(i + 1) * 128, :])

    W1 = []
    for wi_, (wsrc_, ntil_, wid_) in enumerate(((wg_sh, 8, 1024), (wu_sh, 8, 1024), (wd_sh, 32, 256))):
        for it_ in range(ntil_):
            W1.append((wsrc_, wi_, it_, wid_))
    NW1 = len(W1)
    wstP = [regABf[:, 16608 + 2056 * j:16608 + 2056 * j + 2048] for j in range(6)]
    asum4 = regABf[:, 28960:28964]
    absacc3 = regABf[:, 28968:28971]
    wdum = [regAB[:, 58240 + 2048 * j:58240 + 2048 * (j + 1)] for j in range(2)]
    s_wl1 = [Sem(nc, es, "wl1_%d" % j) for j in range(6)]
    s_p = Sem(nc, es, "pacc")
    def w1_src(wsrc, it, wid):
        return wsrc.rearrange("(n p) f -> n p f", p=128)[2 * it:2 * it + 2]
    def w1_dst(j, wid):
        return wstP[j][:, 0:2 * wid].rearrange("p (n f) -> p n f", n=2)
    pz29 = s_p.inc()
    b.emit("gpsimd", lambda e, h=s_p.h: e.memset(absacc3[:], 0.0).then_inc(h, 1))
    w1_abs = {}
    w1_acc = {}
    for gi in range(6):
        wsrc_, wi_, it_, wid_ = W1[gi]
        b.dma("sync", s_wl1[gi], w1_dst(gi, wid_), w1_src(wsrc_, it_, wid_))
    w1_gi = [0]
    def emit_w1_step():
        gi = w1_gi[0]
        if gi >= NW1:
            return
        w1_gi[0] += 1
        wsrc_, wi_, it_, wid_ = W1[gi]
        j = gi % 6
        b.wait("scalar", s_wl1[j], s_wl1[j].n)
        b.wait("scalar", s_p, w1_acc[gi - 4] if gi >= 4 else pz29)
        ss = s_s.inc()
        b.emit("scalar", lambda e, j=j, wid=wid_, q=gi % 4, h=s_s.h: e.activation(
            out=wdum[j % 2][:, 0:2 * wid], in_=wstP[j][:, 0:2 * wid], func=AF.Abs,
            accum_out=asum4[:, q:q + 1]).then_inc(h, 1))
        w1_abs[gi] = ss
        b.wait("gpsimd", s_s, ss)
        pp = s_p.inc()
        def w1acc(e, wi=wi_, q=gi % 4, h=s_p.h):
            e.drain()
            e.tensor_add(out=absacc3[:, wi:wi + 1], in0=absacc3[:, wi:wi + 1],
                         in1=asum4[:, q:q + 1]).then_inc(h, 1)
        b.emit("gpsimd", w1acc)
        w1_acc[gi] = pp
        if gi + 6 < NW1:
            wsrc2, wi2, it2, wid2 = W1[gi + 6]
            b.wait("sync", s_s, ss)
            b.dma("sync", s_wl1[j], w1_dst(j, wid2), w1_src(wsrc2, it2, wid2))

    tdone = {}
    b.wait("vector", s_ld, s_ld.n)
    for i in range(ND):
        j = i % 2
        if i >= 2:
            b.wait("vector", s_t, tdone[i - 2])
        v1 = s_v.inc()
        b.emit("vector", lambda e, i=i, j=j, h=s_v.h: e.tensor_copy(out=xbf[j][:], in_=xtmp[i][:]).then_inc(h, 1))
        b.wait("tensor", s_v, v1)
        t1 = s_t.inc()
        for ci, ((c0, cw), pt) in enumerate(zip(TCH, (pb[0], pb[1], pb[4]))):
            b.emit("tensor", lambda e, pt=pt, c0=c0, cw=cw, i=i, j=j, last=(ci == 2), h=s_t.h: (
                e.matmul(pt[:, 0:cw], lhsT=ones_sb[:], rhs=xbf[j][:, c0:c0 + cw], start=(i == 0), stop=(i == ND - 1)).then_inc(h, 1) if last else
                e.matmul(pt[:, 0:cw], lhsT=ones_sb[:], rhs=xbf[j][:, c0:c0 + cw], start=(i == 0), stop=(i == ND - 1))))
        b.wait("vector", s_t, t1)
        v2 = s_v.inc()
        b.emit("vector", lambda e, i=i, j=j, h=s_v.h: e.tensor_mul(out=xbf[j][:], in0=xtmp[i][:], in1=xtmp[i][:]).then_inc(h, 1))
        b.wait("tensor", s_v, v2)
        t2 = s_t.inc()
        for ci, ((c0, cw), pt) in enumerate(zip(TCH, (pb[2], pb[3], pb[5]))):
            b.emit("tensor", lambda e, pt=pt, c0=c0, cw=cw, i=i, j=j, last=(ci == 2), h=s_t.h: (
                e.matmul(pt[:, 0:cw], lhsT=ones_sb[:], rhs=xbf[j][:, c0:c0 + cw], start=(i == 0), stop=(i == ND - 1)).then_inc(h, 1) if last else
                e.matmul(pt[:, 0:cw], lhsT=ones_sb[:], rhs=xbf[j][:, c0:c0 + cw], start=(i == 0), stop=(i == ND - 1))))
        tdone[i] = t2

    inv_d = float(np.float32(1.0 / D))
    b.wait("vector", s_t, s_t.n)
    def fin_stats(e, h=s_v.h):
        e.tensor_scalar(out=mbc[:, 0:512], in0=pb[0][:], scalar1=inv_d, scalar2=None, op0=ALU.mult)
        e.tensor_scalar(out=mbc[:, 512:1024], in0=pb[1][:], scalar1=inv_d, scalar2=None, op0=ALU.mult)
        e.tensor_scalar(out=mbc[:, 1024:1027], in0=pb[4][:, 0:3], scalar1=inv_d, scalar2=None, op0=ALU.mult)
        e.tensor_scalar(out=rbc[:, 0:512], in0=pb[2][:], scalar1=inv_d, scalar2=None, op0=ALU.mult)
        e.tensor_scalar(out=rbc[:, 512:1024], in0=pb[3][:], scalar1=inv_d, scalar2=None, op0=ALU.mult)
        e.tensor_scalar(out=rbc[:, 1024:1027], in0=pb[5][:, 0:3], scalar1=inv_d, scalar2=None, op0=ALU.mult)
        e.drain()
        e.tensor_mul(out=cv0[:], in0=mbc[:], in1=mbc[:])
        e.drain()
        e.tensor_sub(out=rbc[:], in0=rbc[:], in1=cv0[:]).then_inc(h, 1)
    v_stats = s_v.inc()
    b.emit("vector", fin_stats)
    b.wait("scalar", s_v, v_stats)
    def fin_rstd(e, h=s_s.h):
        e.activation(out=rbc[:], in_=rbc[:], func=AF.Ln, bias=float(EPS), scale=1.0)
        e.drain()
        e.activation(out=rbc[:], in_=rbc[:], func=AF.Exp, bias=0.0, scale=-0.5).then_inc(h, 1)
    s_rstd = s_s.inc()
    b.emit("scalar", fin_rstd)
    b.wait("vector", s_s, s_rstd)
    for i in range(ND):
        def lnx_tile(e, i=i, h=s_v.h):
            e.drain()
            e.tensor_sub(out=cv0[:], in0=xtmp[i][:], in1=mbc[:])
            e.drain()
            e.tensor_mul(out=lnx1[i][:], in0=cv0[:], in1=rbc[:]).then_inc(h, 1)
        s_v.inc()
        b.emit("vector", lnx_tile)
    v_lnx1 = s_v.n

    if "lnx1" in debug:
        dbgl = b.dbg("lnx1", [D, TH])
        b.wait("gpsimd", s_v, v_lnx1)
        for i in range(ND):
            b.dma("gpsimd", s_gst, dbgl[i * 128:(i + 1) * 128, :], lnx1[i][:])

    b.barrier()

    def panel_src(w, f0, fw=128):
        return w[:, f0:f0 + fw].rearrange("(k p) f -> p k f", p=128)
    def panel_dst(j, fw=128):
        return wpan[j].ap().rearrange("p (k f) -> p k f", f=128)[:, :, 0:fw]

    s_wpb = [Sem(nc, es, "wp0"), Sem(nc, es, "wp1")]
    b.dma("gpsimd", s_wpb[0], panel_dst(0), panel_src(w_xbc, 0))
    NT_ALL = NF_XBC + 1
    evd = {}
    tmm = {}
    xpr = xpart.rearrange("p (c i t) -> p c i t", c=NCH, i=NF_G)
    for fi in range(NT_ALL):
        j = fi % 2
        r = fi % 2
        if fi + 1 < NT_ALL:
            jn = (fi + 1) % 2
            if fi >= 1:
                b.wait("gpsimd", s_t, tmm[fi - 1])
            if fi + 1 < NF_XBC:
                b.dma("gpsimd", s_wpb[jn], panel_dst(jn), panel_src(w_xbc, (fi + 1) * 128))
            else:
                b.dma("gpsimd", s_wpb[jn], panel_dst(jn, 64), panel_src(w_dt, 0, 64))
        b.wait("tensor", s_wpb[j], s_wpb[j].n)
        if fi == 0:
            b.wait("tensor", s_v, v_lnx1)
        if fi >= 2:
            b.wait("tensor", s_s, evd[fi - 2])
        fw = 128 if fi < NF_XBC else 64
        pA, pB, pT = pb[3 * r], pb[3 * r + 1], pb[3 * r + 2]
        ti = s_t.inc()
        for k in range(ND):
            lt = wpan[j][:, k * 128:k * 128 + fw]
            st = (k == 0); sp = (k == ND - 1)
            b.emit("tensor", lambda e, lt=lt, k=k, st=st, sp=sp, pA=pA, fw=fw: e.matmul(
                pA[0:fw, 0:512], lhsT=lt, rhs=lnx1[k][:, 0:512], start=st, stop=sp))
            b.emit("tensor", lambda e, lt=lt, k=k, st=st, sp=sp, pB=pB, fw=fw: e.matmul(
                pB[0:fw, 0:512], lhsT=lt, rhs=lnx1[k][:, 512:1024], start=st, stop=sp))
            b.emit("tensor", lambda e, lt=lt, k=k, st=st, sp=sp, pT=pT, fw=fw, last=(k == ND - 1), h=s_t.h: (
                e.matmul(pT[0:fw, 0:3], lhsT=lt, rhs=lnx1[k][:, 1024:1027], start=st, stop=sp).then_inc(h, 1) if last else
                e.matmul(pT[0:fw, 0:3], lhsT=lt, rhs=lnx1[k][:, 1024:1027], start=st, stop=sp)))
        tmm[fi] = ti
        b.wait("scalar", s_t, ti)
        if fi >= 1:
            b.wait("scalar", s_v, cvd_prev)
        if fi < NF_XBC:
            def evac(e, pA=pA, pB=pB, pT=pT, h=s_s.h):
                e.copy(out=cv0[:, 0:512], in_=pA[:])
                e.copy(out=cv0[:, 512:1024], in_=pB[:])
                e.copy(out=cv0[:, 1024:1027], in_=pT[:, 0:3]).then_inc(h, 1)
            ev = s_s.inc()
            b.emit("scalar", evac)
            evd[fi] = ev
            b.wait("vector", s_s, ev)
            def conv(e, fi=fi, h=s_v.h):
                e.tensor_scalar(out=cv1[:], in0=cv0[:, 0:T], scalar1=cwt[:, 4 * fi:4 * fi + 1], scalar2=None, op0=ALU.mult)
                for k in range(1, 4):
                    e.drain()
                    e.scalar_tensor_tensor(out=cv1[:], in0=cv0[:, k:k + T], scalar=cwt[:, 4 * fi + k:4 * fi + k + 1],
                                           in1=cv1[:], op0=ALU.mult, op1=ALU.add)
                e.drain().then_inc(h, 1)
            cvd = s_v.inc()
            cvd_prev = cvd
            b.emit("vector", conv)
            b.wait("scalar", s_v, cvd)
            if fi < NF_G:
                dst = xpr[:, :, fi, :]
            elif fi == NF_G:
                dst = Bc[:]
            else:
                dst = Cc[:]
            def silu(e, fi=fi, dst=dst, h=s_s.h):
                e.activation(out=dst, in_=cv1[:], func=AF.Silu, bias=cbt[:, fi:fi + 1], scale=1.0).then_inc(h, 1)
            s_s.inc()
            b.emit("scalar", silu)
        else:
            def evac_dt(e, pA=pA, pB=pB, pT=pT, h=s_s.h):
                e.activation(out=dt32[:, 0:512], in_=pA[0:64, :], func=AF.Identity, bias=dtbt[:], scale=1.0)
                e.activation(out=dt32[:, 512:1024], in_=pB[0:64, :], func=AF.Identity, bias=dtbt[:], scale=1.0)
                e.activation(out=dt32[:, 1024:1027], in_=pT[0:64, 0:3], func=AF.Identity, bias=dtbt[:], scale=1.0).then_inc(h, 1)
            ev = s_s.inc()
            b.emit("scalar", evac_dt)
            evd[fi] = ev
            b.wait("vector", s_s, ev)
            vv = s_v.inc()
            def dtmin(e, h=s_v.h):
                e.tensor_scalar(out=cv0[0:64, 0:TH], in0=dt32[:], scalar1=10.0, scalar2=None, op0=ALU.min).then_inc(h, 1)
            b.emit("vector", dtmin)
            b.wait("scalar", s_v, vv)
            def soft(e, h=s_s.h):
                e.activation(out=cv0[0:64, 0:TH], in_=cv0[0:64, 0:TH], func=AF.Exp)
                e.drain()
                e.activation(out=cv0[0:64, 0:TH], in_=cv0[0:64, 0:TH], func=AF.Ln, bias=1.0).then_inc(h, 1)
            sv = s_s.inc()
            b.emit("scalar", soft)
            b.wait("vector", s_s, sv)
            s_v.inc()
            def dtmax(e, h=s_v.h):
                e.tensor_max(out=dt32[:], in0=dt32[:], in1=cv0[0:64, 0:TH]).then_inc(h, 1)
            b.emit("vector", dtmax)
        emit_w1_step()
        if fi % 2 == 0:
            emit_w1_step()
    while w1_gi[0] < NW1:
        emit_w1_step()

    b.barrier()

    WSH = [(wg_sh, wqg_in, 16, 1024, 0), (wu_sh, wqu_in, 16, 1024, 1), (wd_sh, wqd_in, 64, 256, 2)]
    s_wl = [Sem(nc, es, "wl0"), Sem(nc, es, "wl1")]
    def shard_src(wsrc, it, wid):
        if wid == 1024:
            return wsrc[it * 128:(it + 1) * 128, :]
        return wsrc.rearrange("(n p) o -> n p o", p=128)[it, :, :]
    vz29 = s_v.inc()
    def z29(e, h=s_v.h):
        e.memset(mrow[:], 0.0).then_inc(h, 1)
    b.emit("vector", z29)
    b.wait("tensor", s_v, vz29)
    b.wait("tensor", s_p, s_p.n)
    b.wait("scalar", s_v, vz29)
    tps = s_t.inc()
    def psum3(e, h=s_t.h):
        e.matmul(pb[7][0:1, 0:1], lhsT=absacc3[:, 0:1], rhs=onescol[:], start=True, stop=True)
        e.matmul(pb[7][0:1, 1:2], lhsT=absacc3[:, 1:2], rhs=onescol[:], start=True, stop=True)
        e.matmul(pb[7][0:1, 2:3], lhsT=absacc3[:, 2:3], rhs=onescol[:], start=True, stop=True).then_inc(h, 1)
    b.emit("tensor", psum3)
    b.wait("scalar", s_t, tps)
    sps = s_s.inc()
    b.emit("scalar", lambda e, h=s_s.h: e.copy(out=mrow[:, 0:3], in_=pb[7][0:1, 0:3]).then_inc(h, 1))
    b.wait("sync", s_s, sps)
    b.dma("sync", s_wl[0], ar_in.ap(), mrow[:, 0:4])
    b.wait("gpsimd", s_wl[0], s_wl[0].n)
    s_cw = Sem(nc, es, "cw")
    s_cw.inc()
    b.emit("gpsimd", lambda e, h=s_cw.h: e.collective_compute(
        "AllReduce", ALU.add, replica_groups=[[0, 1, 2, 3, 4, 5, 6, 7]],
        ins=[ar_in.ap()], outs=[ar_out.ap()]).then_inc(h, 1))
    b.wait("sync", s_cw, s_cw.n)
    b.dma("sync", s_wl[1], mrow[:, 4:8], ar_out.ap())
    b.wait("vector", s_wl[1], s_wl[1].n)
    vmw = s_v.inc()
    NELG = float(FFI * D)
    def mkthr(e, h=s_v.h):
        e.tensor_scalar(out=mrow[:, 12:16], in0=mrow[:, 4:8], scalar1=1.0 / NELG, scalar2=float(EPS), op0=ALU.mult, op1=ALU.max)
        e.drain()
        e.reciprocal(out=mrow[:, 8:12], in_=mrow[:, 12:16]).then_inc(h, 1)
    b.emit("vector", mkthr)
    b.wait("tensor", s_v, vmw)
    tbc = s_t.inc()
    b.emit("tensor", lambda e, h=s_t.h: e.matmul(pb[7][:, 0:8], lhsT=ones1f[0:1, :], rhs=mrow[0:1, 8:16], start=True, stop=True).then_inc(h, 1))
    b.wait("scalar", s_t, tbc)
    swq = s_s.inc()
    b.emit("scalar", lambda e, h=s_s.h: e.copy(out=wqrow[:], in_=pb[7][:, 0:8]).then_inc(h, 1))
    b.wait("vector", s_s, swq)
    s_cg = Sem(nc, es, "cg")
    b.barrier()

    SCB = 32768
    S_sb = regAB[:, SCB:SCB + HEADS * P]
    EGk = regAB[:, SCB + 4096:SCB + 4096 + HEADS * NCH]
    scoresM = regAB[:, SCB + 4608:SCB + 4608 + Q]
    Btm = regAB[:, SCB + 4736:SCB + 4736 + Q]
    dtx4 = [regAB[:, SCB + 4864 + 256 * j: SCB + 5120 + 256 * j] for j in range(2)]
    dtxw4 = [regAB[:, SCB + 5376 + 256 * j: SCB + 5632 + 256 * j] for j in range(2)]
    SCBf = 16384
    ytmp4 = regABf[:, SCBf + 2944:SCBf + 2944 + 256]
    zer64 = regABf[0:64, SCBf + 3200:SCBf + 3200 + Q]
    afm = regABf[0:64, SCBf + 3328:SCBf + 3328 + Q]
    cumfm = regABf[0:64, SCBf + 3456:SCBf + 3456 + Q]
    cumflat = regABf[0:33, SCBf + 3584:SCBf + 3584 + 2048]
    DTm4 = [regABf[:, SCBf + 5632 + 512 * j: SCBf + 6144 + 512 * j] for j in range(2)]
    M4 = [regAB[:, SCB + 13312 + 512 * j: SCB + 13824 + 512 * j] for j in range(2)]
    gstg = [b.sb(f"gstg{j}", [128, T], BF16, D0 + 9824 + 2048 * j) for j in range(2)]
    x_tm = regE[:, 0:INTER]
    dttm = regEf[:, 2048:2112]
    cumtm = regEf[:, 2112:2176]
    ncumtm = regEf[:, 2176:2240]
    Etm = regEf[:, 2240:2304]
    wtm = regEf[:, 2304:2368]
    etbc = regEf[:, 2368:2432]
    g_run = regEf[:, 2432:2496]
    W4 = [regE[:, 4992 + 512 * j: 5504 + 512 * j] for j in range(2)]
    ytm = [regAB[:, c * 4096:(c + 1) * 4096] for c in range(NCH)]

    b.wait("vector", s_ld, s_ld.n)
    vz = s_v.inc()
    def scan_init(e, h=s_v.h):
        e.memset(zer64[:], 0.0)
        e.memset(S_sb[:], 0.0)
        e.memset(g_run[:], 1.0).then_inc(h, 1)
    b.emit("vector", scan_init)
    sA = s_s.inc()
    def mkA(e, h=s_s.h):
        e.activation(out=abtA[:], in_=abt[:], func=AF.Exp).then_inc(h, 1)
    b.emit("scalar", mkA)
    b.wait("vector", s_s, sA)
    vA = s_v.inc()
    def negA(e, h=s_v.h):
        e.tensor_scalar(out=abtA[:], in0=abtA[:], scalar1=-1.0, scalar2=None, op0=ALU.mult).then_inc(h, 1)
    b.emit("vector", negA)
    b.wait("tensor", s_ld, s_ld.n)
    b.wait("tensor", s_v, vz)
    tdb = s_t.inc()
    b.emit("tensor", lambda e, h=s_t.h: e.matmul(pb[7][:, 0:64], lhsT=ones1f[0:1, :], rhs=dvr[:], start=True, stop=True).then_inc(h, 1))
    b.wait("scalar", s_t, tdb)
    sdb = s_s.inc()
    b.emit("scalar", lambda e, h=s_s.h: e.copy(out=Dbc[:], in_=pb[7][:, 0:64]).then_inc(h, 1))
    b.wait("tensor", s_s, sdb)
    b.barrier()


    def amax_bcast(amr_ap, out_ap):
        b.wait("tensor", s_v, s_v.n)
        for c in range(8):
            pk = pb[6 + c % 2]
            if c >= 2:
                b.wait("tensor", s_v, ab_v[c - 2])
            tv = s_t.inc()
            b.emit("tensor", lambda e, c=c, pk=pk, amr_ap=amr_ap, h=s_t.h: e.transpose(
                pk[:, 0:128], in_=amr_ap[:, 128 * c:128 * c + 128], identity=identf[:]).then_inc(h, 1))
            b.wait("vector", s_t, tv)
            vv = s_v.inc()
            def red(e, c=c, pk=pk, h=s_v.h):
                e.drain()
                e.reduce_max(out=amtm[:, c:c + 1], in_=pk[:, 0:128], axis=mybir.AxisListType.X).then_inc(h, 1)
            b.emit("vector", red)
            if c == 0:
                ab_v = {}
            ab_v[c] = vv
        b.wait("vector", s_v, s_v.n)
        ab_t = {}
        for c in range(8):
            pk = pb[6 + c % 2]
            if c >= 2:
                b.wait("vector", s_s, ab_s[c - 2])
            if c >= 1:
                b.wait("vector", s_t, ab_t[c - 1])
            vv = s_v.inc()
            def colb(e, c=c, h=s_v.h):
                e.drain()
                e.tensor_scalar(out=coltl[:], in0=ones_sb[:], scalar1=amtm[:, c:c + 1], scalar2=None, op0=ALU.mult).then_inc(h, 1)
            b.emit("vector", colb)
            b.wait("tensor", s_v, vv)
            tv = s_t.inc()
            ab_t[c] = tv
            b.emit("tensor", lambda e, pk=pk, h=s_t.h: e.transpose(pk[:, 0:128], in_=coltl[:], identity=identf[:]).then_inc(h, 1))
            b.wait("scalar", s_t, tv)
            sv = s_s.inc()
            b.emit("scalar", lambda e, c=c, pk=pk, out_ap=out_ap, h=s_s.h: e.copy(
                out=out_ap[:, 128 * c:128 * c + 128], in_=pk[:, 0:128]).then_inc(h, 1))
            if c == 0:
                ab_s = {}
            ab_s[c] = sv
        b.wait("vector", s_s, s_s.n)

    s_cf = Sem(nc, es, "cf")
    pb2bf = [pb[i][:].bitcast(BF16) for i in range(8)]

    pm = {"vTS": [0] * 16, "vEv": [0] * 16, "sExp": [0] * 16, "vW": [0] * 16, "tY": [0] * 16,
          "tP4": 0, "tP8": 0, "cf": 0}

    def scan_chunk(c):
        dts = dt32[:, 3 + c * Q: 3 + (c + 1) * Q]
        Bsl = Bc[:, c * Q:(c + 1) * Q]
        Csl = Cc[:, c * Q:(c + 1) * Q]
        vTS = [0] * 16; vEv = [0] * 16
        if pm["vEv"][15]:
            b.wait("scalar", s_v, pm["vEv"][15])
        if pm["tP8"]:
            b.wait("scalar", s_t, pm["tP8"])
        if pm["tP4"]:
            b.wait("vector", s_t, pm["tP4"])
        if pm["cf"]:
            b.wait("vector", s_cf, pm["cf"])
        vP1 = s_v.inc()
        def p1(e, dts=dts, h=s_v.h):
            e.tensor_scalar(out=afm[:], in0=dts, scalar1=abtA[:], scalar2=None, op0=ALU.mult)
            e.drain()
            e.tensor_tensor_scan(out=cumfm[:], data0=afm[:], data1=zer64[:], initial=0.0, op0=ALU.add, op1=ALU.add).then_inc(h, 1)
        b.emit("vector", p1)
        b.wait("sync", s_v, vP1)
        b.dma("sync", s_cf, cumflat[0:1, :].rearrange("p (h t) -> p h t", t=Q), cumfm[0:16, :])
        b.dma("sync", s_cf, cumflat[32:33, :].rearrange("p (h t) -> p h t", t=Q), cumfm[16:32, :])
        cfA = s_cf.n
        trs_ev = {}
        for rnd in range(4):
            bk = rnd % 2
            if rnd < 2:
                b.wait("tensor", s_v, pm["vEv"][14 + bk] or 0) if pm["vEv"][14 + bk] else None
            else:
                b.wait("tensor", s_s, trs_ev[rnd - 2])
            tr = s_t.inc()
            for ii in range(8):
                i = rnd * 8 + ii
                b.emit("tensor", lambda e, c=c, i=i, ii=ii, bk=bk, last=(ii == 7), h=s_t.h: (
                    e.transpose(pb2bf[bk][:, 128 * ii:128 * ii + 128], in_=xpr[:, c, i, :], identity=identb[:]).then_inc(h, 1)
                    if last else
                    e.transpose(pb2bf[bk][:, 128 * ii:128 * ii + 128], in_=xpr[:, c, i, :], identity=identb[:])))
            b.wait("scalar", s_t, tr)
            sv = s_s.inc()
            b.emit("scalar", lambda e, rnd=rnd, bk=bk, h=s_s.h: e.copy(
                out=x_tm[:, 1024 * rnd:1024 * rnd + 1024], in_=pb2bf[bk][:, 0:1024]).then_inc(h, 1))
            trs_ev[rnd] = sv
        if pm["vTS"][14]:
            b.wait("tensor", s_v, pm["vTS"][14])
        b.wait("tensor", s_v, vP1)
        tP4 = s_t.inc()
        def p4(e, dts=dts, h=s_t.h):
            e.transpose(pb[2][:, 0:64], in_=dts, identity=identf[0:64, 0:64])
            e.transpose(pb[2][:, 64:128], in_=cumfm[:], identity=identf[0:64, 0:64]).then_inc(h, 1)
        b.emit("tensor", p4)
        b.wait("scalar", s_t, tP4)
        sP5 = s_s.inc()
        def p5(e, h=s_s.h):
            e.copy(out=dttm[:], in_=pb[2][:, 0:64])
            e.copy(out=cumtm[:], in_=pb[2][:, 64:128])
            e.drain()
            e.activation(out=Etm[:], in_=cumtm[:], func=AF.Exp).then_inc(h, 1)
        b.emit("scalar", p5)
        b.wait("vector", s_s, sP5)
        vP6 = s_v.inc()
        def p6(e, h=s_v.h):
            e.drain()
            e.tensor_scalar(out=ncumtm[:], in0=cumtm[:], scalar1=-1.0, scalar2=None, op0=ALU.mult).then_inc(h, 1)
        b.emit("vector", p6)
        if pm["vTS"][15]:
            b.wait("tensor", s_v, pm["vTS"][15])
        b.wait("tensor", s_s, sP5)
        tP8 = s_t.inc()
        def p8(e, Bsl=Bsl, Csl=Csl, h=s_t.h):
            e.matmul(pb[3][:, 0:128], lhsT=Bsl, rhs=Csl, start=True, stop=True)
            e.matmul(pb[3][:, 128:192], lhsT=sel127[:], rhs=cumtm[:], start=True, stop=True)
            e.transpose(pb2bf[3][:, 384:512], in_=Bsl, identity=identb[:]).then_inc(h, 1)
        b.emit("tensor", p8)
        b.wait("vector", s_t, tP8)
        vP9 = s_v.inc()
        def p9v(e, c=c, h=s_v.h):
            e.tensor_tensor(out=scoresM[:], in0=pb[3][:, 0:128], in1=triu_m[:], op=ALU.mult)
            e.tensor_sub(out=wtm[:], in0=pb[3][:, 128:192], in1=cumtm[:])
            e.tensor_mul(out=EGk[:, 64 * c:64 * c + 64], in0=Etm[:], in1=g_run[:]).then_inc(h, 1)
        b.emit("vector", p9v)
        b.wait("scalar", s_v, vP9)
        sP9 = s_s.inc()
        def p9s(e, h=s_s.h):
            e.activation(out=wtm[:], in_=wtm[:], func=AF.Exp)
            e.activation(out=etbc[:], in_=pb[3][:, 128:192], func=AF.Exp)
            e.copy(out=Btm[:], in_=pb2bf[3][:, 384:512]).then_inc(h, 1)
        b.emit("scalar", p9s)
        b.wait("vector", s_s, sP9)
        vP9b = s_v.inc()
        def p9b(e, h=s_v.h):
            e.drain()
            e.tensor_mul(out=g_run[:], in0=g_run[:], in1=etbc[:]).then_inc(h, 1)
        b.emit("vector", p9b)

        cfB = cfA
        sExp = [0] * 16; vW = [0] * 16; tY = [0] * 16; t1s = [0] * 16
        for g in range(16):
            pi = g % 2
            DTb = pb[2 + pi]; yABb = pb[4 + pi]; dSb = pb[0 + pi]
            if g == 8:
                b.wait("sync", s_t, t1s[7])
                b.dma("sync", s_cf, cumflat[0:1, :].rearrange("p (h t) -> p h t", t=Q), cumfm[32:48, :])
                b.dma("sync", s_cf, cumflat[32:33, :].rearrange("p (h t) -> p h t", t=Q), cumfm[48:64, :])
                cfB = s_cf.n
            b.wait("tensor", s_cf, cfA if g < 8 else cfB)
            if g >= 2:
                b.wait("tensor", s_v, vTS[g - 2])
            elif pm["vTS"][g + 14]:
                b.wait("tensor", s_v, pm["vTS"][g + 14])
            gq = g % 8
            base = 0 if gq < 4 else 32
            col = 512 * (gq % 4)
            t1 = s_t.inc()
            b.emit("tensor", lambda e, DTb=DTb, base=base, col=col, h=s_t.h: e.matmul(
                DTb[:], lhsT=ones1f[base:base + 1, :], rhs=cumflat[base:base + 1, col:col + 512], start=True, stop=True).then_inc(h, 1))
            t1s[g] = t1
            b.wait("vector", s_t, t1)
            if g >= 2:
                b.wait("vector", s_s, sExp[g - 2])
                b.wait("vector", s_t, tY[g - 2])
            else:
                if pm["sExp"][g + 14]:
                    b.wait("vector", s_s, pm["sExp"][g + 14])
                if pm["tY"][g + 14]:
                    b.wait("vector", s_t, pm["tY"][g + 14])
            b.wait("vector", s_s, sP9)
            v1 = s_v.inc()
            def st2(e, g=g, pi=pi, DTb=DTb, h=s_v.h):
                for h4 in range(4):
                    hh = 4 * g + h4
                    e.tensor_scalar(out=DTm4[pi][:, 128 * h4:128 * h4 + 128], in0=DTb[:, 128 * h4:128 * h4 + 128],
                                    scalar1=ncumtm[:, hh:hh + 1], scalar2=0.0, op0=ALU.add, op1=ALU.min)
                e.tensor_tensor(out=dtx4[pi].rearrange("p (g4 q) -> p g4 q", q=64),
                                in0=x_tm[:, 256 * g:256 * g + 256].rearrange("p (g4 q) -> p g4 q", q=64),
                                in1=dttm[:, 4 * g:4 * g + 4].unsqueeze(2).broadcast_to([128, 4, 64]), op=ALU.mult)
                e.drain()
                e.tensor_tensor(out=dtxw4[pi].rearrange("p (g4 q) -> p g4 q", q=64),
                                in0=dtx4[pi].rearrange("p (g4 q) -> p g4 q", q=64),
                                in1=wtm[:, 4 * g:4 * g + 4].unsqueeze(2).broadcast_to([128, 4, 64]), op=ALU.mult)
                e.drain().then_inc(h, 1)
            b.emit("vector", st2)
            vTS[g] = v1
            b.wait("scalar", s_v, v1)
            if g >= 2:
                b.wait("scalar", s_v, vW[g - 2])
            elif pm["vW"][g + 14]:
                b.wait("scalar", s_v, pm["vW"][g + 14])
            s1 = s_s.inc()
            b.emit("scalar", lambda e, pi=pi, h=s_s.h: e.activation(out=M4[pi][:], in_=DTm4[pi][:], func=AF.Exp).then_inc(h, 1))
            sExp[g] = s1
            b.wait("vector", s_s, s1)
            v2 = s_v.inc()
            def st4(e, pi=pi, h=s_v.h):
                e.tensor_tensor(out=W4[pi].rearrange("p (g4 q) -> p g4 q", q=128),
                                in0=M4[pi].rearrange("p (g4 q) -> p g4 q", q=128),
                                in1=scoresM[:].unsqueeze(1).broadcast_to([128, 4, 128]), op=ALU.mult)
                e.drain().then_inc(h, 1)
            b.emit("vector", st4)
            vW[g] = v2
            b.wait("tensor", s_v, v2)
            if g >= 2:
                b.wait("tensor", s_v, vEv[g - 2])
            elif pm["vEv"][g + 14]:
                b.wait("tensor", s_v, pm["vEv"][g + 14])
            t2 = s_t.inc()
            def st5(e, g=g, pi=pi, yABb=yABb, dSb=dSb, Csl=Csl, h=s_t.h):
                for h4 in range(4):
                    hh = 4 * g + h4
                    e.matmul(yABb[:, 64 * h4:64 * h4 + 64], lhsT=W4[pi][:, 128 * h4:128 * h4 + 128],
                             rhs=dtx4[pi][:, 64 * h4:64 * h4 + 64], start=True, stop=True)
                    e.matmul(yABb[:, 256 + 64 * h4:256 + 64 * h4 + 64], lhsT=Csl,
                             rhs=S_sb[:, 64 * hh:64 * hh + 64], start=True, stop=True)
                    ins = e.matmul(dSb[:, 64 * h4:64 * h4 + 64], lhsT=Btm[:],
                                   rhs=dtxw4[pi][:, 64 * h4:64 * h4 + 64], start=True, stop=True)
                    if h4 == 3:
                        ins.then_inc(h, 1)
            b.emit("tensor", st5)
            tY[g] = t2
            emit_gate_piece()
            b.wait("vector", s_t, t2)
            v3 = s_v.inc()
            def st6(e, c=c, g=g, pi=pi, yABb=yABb, dSb=dSb, h=s_v.h):
                e.drain()
                for h4 in range(4):
                    hh = 4 * g + h4
                    e.tensor_scalar(out=ytmp4[:, 64 * h4:64 * h4 + 64], in0=yABb[:, 256 + 64 * h4:256 + 64 * h4 + 64],
                                    scalar1=Etm[:, hh:hh + 1], scalar2=None, op0=ALU.mult)
                e.drain()
                e.tensor_add(out=ytmp4[:], in0=ytmp4[:], in1=yABb[:, 0:256])
                e.drain()
                for h4 in range(4):
                    hh = 4 * g + h4
                    e.scalar_tensor_tensor(out=ytm[c][:, 64 * hh:64 * hh + 64], in0=x_tm[:, 64 * hh:64 * hh + 64],
                                           scalar=Dbc[:, hh:hh + 1], in1=ytmp4[:, 64 * h4:64 * h4 + 64],
                                           op0=ALU.mult, op1=ALU.add)
                for h4 in range(4):
                    hh = 4 * g + h4
                    e.tensor_scalar(out=S_sb[:, 64 * hh:64 * hh + 64], in0=S_sb[:, 64 * hh:64 * hh + 64],
                                    scalar1=etbc[:, hh:hh + 1], scalar2=None, op0=ALU.mult)
                e.drain()
                e.tensor_add(out=S_sb[:, 256 * g:256 * g + 256], in0=S_sb[:, 256 * g:256 * g + 256], in1=dSb[:, 0:256]).then_inc(h, 1)
            b.emit("vector", st6)
            vEv[g] = v3
        pm["vTS"] = vTS; pm["vEv"] = vEv; pm["sExp"] = sExp; pm["vW"] = vW; pm["tY"] = tY
        pm["tP4"] = tP4; pm["tP8"] = tP8; pm["cf"] = s_cf.n

    s_gs = [Sem(nc, es, "gs0"), Sem(nc, es, "gs1")]
    gstate = {"fi": 0, "q": 0}
    gmk = {"t": {}, "s": {}}
    b.wait("gpsimd", s_t, tmm[NT_ALL - 1])
    b.dma("gpsimd", s_wpb[0], panel_dst(0), panel_src(w_z, 0))
    b.dma("gpsimd", s_wpb[1], panel_dst(1), panel_src(w_z, 128))
    gp_cnt = [s_wpb[0].n, s_wpb[1].n]

    def emit_gate_piece():
        fi, q = gstate["fi"], gstate["q"]
        if fi >= NF_G:
            return
        j = fi % 2
        if q == 0:
            b.wait("tensor", s_wpb[j], gp_cnt[j])
            if fi >= 1:
                b.wait("tensor", s_s, gmk["s"][fi - 1])
            if fi >= 2:
                b.wait("tensor", s_gs[j], s_gs[j].n)
        tmark = s_t.inc()
        for k in range(4 * q, 4 * q + 4):
            st = (k == 0); sp = (k == ND - 1)
            b.emit("tensor", lambda e, j=j, k=k, st=st, sp=sp: e.matmul(
                pb[6][:], lhsT=wpan[j][:, k * 128:(k + 1) * 128], rhs=lnx1[k][:, 3:515], start=st, stop=sp))
            b.emit("tensor", lambda e, j=j, k=k, st=st, sp=sp, last=(k == 4 * q + 3), h=s_t.h: (
                e.matmul(pb[7][:], lhsT=wpan[j][:, k * 128:(k + 1) * 128], rhs=lnx1[k][:, 515:1027], start=st, stop=sp).then_inc(h, 1) if last else
                e.matmul(pb[7][:], lhsT=wpan[j][:, k * 128:(k + 1) * 128], rhs=lnx1[k][:, 515:1027], start=st, stop=sp)))
        if q == 3:
            gmk["t"][fi] = tmark
            b.wait("scalar", s_t, tmark)
            sm = s_s.inc()
            def gevac(e, fi=fi, j=j, h=s_s.h):
                e.copy(out=gstg[j][:, 0:512], in_=pb[6][:])
                e.copy(out=gstg[j][:, 512:1024], in_=pb[7][:]).then_inc(h, 1)
            b.emit("scalar", gevac)
            gmk["s"][fi] = sm
            b.wait("sync", s_s, sm)
            b.dma("sync", s_gs[j], gate_dram[fi * 128:(fi + 1) * 128, :], gstg[j][:])
            if fi + 2 < NF_G:
                b.wait("gpsimd", s_t, tmark)
                jn = (fi + 2) % 2
                b.dma("gpsimd", s_wpb[jn], panel_dst(jn), panel_src(w_z, (fi + 2) * 128))
                gp_cnt[jn] = s_wpb[jn].n
            gstate["fi"] += 1
            gstate["q"] = 0
        else:
            gstate["q"] += 1

    for c in range(NCH):
        scan_chunk(c)
    while gstate["fi"] < NF_G:
        emit_gate_piece()
    b.wait("sync", s_gs[0], s_gs[0].n)
    b.wait("sync", s_gs[1], s_gs[1].n)

    b.barrier()

    agS_in = nc.dram_tensor("agS_in", [128, HEADS * P], BF16, kind="Internal")
    agS_out = nc.dram_tensor("agS_out", [256, HEADS * P], BF16, kind="Internal")
    S_in = regE[:, 0:HEADS * P]
    b.dma("sync", s_ld, smt[:], smask)
    b.dma("sync", s_st, agS_in.ap(), S_sb[:])
    b.wait("gpsimd", s_st, s_st.n)
    s_cc = Sem(nc, es, "cc")
    s_cc.inc()
    b.emit("gpsimd", lambda e, h=s_cc.h: e.collective_compute(
        "AllGather", ALU.bypass, replica_groups=[[0, 1], [2, 3], [4, 5], [6, 7]],
        ins=[agS_in.ap()], outs=[agS_out.ap()]).then_inc(h, 1))
    b.wait("sync", s_cc, s_cc.n)
    b.dma("sync", s_ld, S_in[:], agS_out.ap()[0:128, :])
    b.wait("vector", s_ld, s_ld.n)
    vsi = s_v.inc()
    def mask_sin(e, h=s_v.h):
        e.tensor_scalar(out=S_in[:], in0=S_in[:], scalar1=smt[:], scalar2=None, op0=ALU.mult).then_inc(h, 1)
    b.emit("vector", mask_sin)
    b.wait("tensor", s_v, vsi)
    corr_v = {}
    for c in range(NCH):
        for g in range(16):
            pk = pb[4 + (g % 2)]
            if c * 16 + g >= 2:
                b.wait("tensor", s_v, corr_v[c * 16 + g - 2])
            ti = s_t.inc()
            def corr_mm(e, c=c, g=g, pk=pk, h=s_t.h):
                for h4 in range(4):
                    hh = 4 * g + h4
                    ins = e.matmul(pk[:, 64 * h4:64 * h4 + 64], lhsT=Cc[:, c * Q:(c + 1) * Q],
                                   rhs=S_in[:, 64 * hh:64 * hh + 64], start=True, stop=True)
                    if h4 == 3:
                        ins.then_inc(h, 1)
            b.emit("tensor", corr_mm)
            b.wait("vector", s_t, ti)
            vv2 = s_v.inc()
            def corr_ev(e, c=c, g=g, pk=pk, h=s_v.h):
                e.drain()
                for h4 in range(4):
                    hh = 4 * g + h4
                    e.scalar_tensor_tensor(out=ytm[c][:, 64 * hh:64 * hh + 64], in0=pk[:, 64 * h4:64 * h4 + 64],
                                           scalar=EGk[:, 64 * c + hh:64 * c + hh + 1],
                                           in1=ytm[c][:, 64 * hh:64 * hh + 64], op0=ALU.mult, op1=ALU.add)
                e.drain().then_inc(h, 1)
            corr_v[c * 16 + g] = vv2
            b.emit("vector", corr_ev)
    b.barrier()

    if "scan" in debug:
        dbgy2 = b.dbg("ytmdbg", [NCH * 128, INTER])
        for c in range(NCH):
            b.dma("gpsimd", s_gst, dbgy2[c * 128:(c + 1) * 128, :], ytm[c][:])
        b.wait("gpsimd", s_gst, s_gst.n)

    gate_t = [regAB[:, SCB + i2 * 1024:SCB + (i2 + 1) * 1024] for i2 in range(NF_G)]
    s_gl = [Sem(nc, es, "gl0"), Sem(nc, es, "gl1")]
    b.dma("sync", s_gl[0], gstg[0][:], gate_dram[0:128, :])
    b.dma("sync", s_gl[1], gstg[1][:], gate_dram[128:256, :])
    silg = regE[:, 0:1024]
    sqt = regE[:, 1024:2048]
    rbcg = regEf[:, 1024:2048]
    resid = [regABf[:, i2 * 1024:(i2 + 1) * 1024] for i2 in range(ND)]
    b.dma("sync", s_ld, nwt[:], nw_in)

    W2 = []
    for wi_, (wsrc_, wdst_, ntil_, wid_, slot_) in enumerate(WSH):
        for it_ in range(ntil_):
            W2.append((wsrc_, wdst_, it_, wid_, slot_, wi_))
    NW2 = len(W2)
    wst2 = [regCf[:, 3104 + 1040 * k:3104 + 1040 * k + 1024] for k in range(3)]
    wqt2 = [regC[:, 12448 + 1024 * k:12448 + 1024 * (k + 1)] for k in range(3)]
    s_wl2 = [Sem(nc, es, "wl2_%d" % k) for k in range(3)]
    s_wq2 = [Sem(nc, es, "wq2_%d" % k) for k in range(3)]
    for gi2 in range(3):
        wsrc_, wdst_, it_, wid_, slot_, wi_ = W2[gi2]
        b.dma("sync", s_wl2[gi2], wst2[gi2][:, 0:wid_], shard_src(wsrc_, it_, wid_))
    w2_gi = [0]
    def emit_w2_step():
        gi = w2_gi[0]
        if gi >= NW2:
            return
        w2_gi[0] += 1
        wsrc_, wdst_, it_, wid_, slot_, wi_ = W2[gi]
        j = gi % 3
        b.wait("scalar", s_wl2[j], s_wl2[j].n)
        ss = s_s.inc()
        b.emit("scalar", lambda e, j=j, wid=wid_, slot=slot_, h=s_s.h: e.activation(
            out=wst2[j][:, 0:wid], in_=wst2[j][:, 0:wid], func=AF.Identity,
            bias=RC, scale=wqrow[:, slot:slot + 1]).then_inc(h, 1))
        b.wait("vector", s_s, ss)
        if gi >= 3:
            b.wait("vector", s_wq2[j], s_wq2[j].n)
        vv = s_v.inc()
        def w2v(e, j=j, wid=wid_, h=s_v.h):
            e.tensor_scalar(out=wst2[j][:, 0:wid], in0=wst2[j][:, 0:wid], scalar1=-RC, scalar2=1.0, op0=ALU.add, op1=ALU.min)
            e.drain()
            e.tensor_scalar(out=wqt2[j][:, 0:wid], in0=wst2[j][:, 0:wid], scalar1=-1.0, scalar2=None, op0=ALU.max).then_inc(h, 1)
        b.emit("vector", w2v)
        b.wait("sync", s_v, vv)
        b.dma("sync", s_wq2[j], shard_src(wdst_.ap(), it_, wid_), wqt2[j][:, 0:wid_])
        if gi + 3 < NW2:
            wsrcN, wdstN, itN, widN, slotN, wiN = W2[gi + 3]
            b.dma("sync", s_wl2[j], wst2[j][:, 0:widN], shard_src(wsrcN, itN, widN))
        if wid_ == 1024 and it_ % 4 == 3:
            p = it_ // 4
            wout = wqg_out if wi_ == 0 else wqu_out
            for k3 in range(3):
                b.wait("gpsimd", s_wq2[k3], s_wq2[k3].n)
            s_cg.inc()
            b.emit("gpsimd", lambda e, win=wdst_, p=p, wout=wout, h=s_cg.h: e.collective_compute(
                "AllGather", ALU.bypass, replica_groups=[[0, 1, 2, 3, 4, 5, 6, 7]],
                ins=[win.ap()[p * 512:(p + 1) * 512, :]],
                outs=[wout.ap()[p * 4096:(p + 1) * 4096, :]]).then_inc(h, 1))
        if wi_ == 2 and it_ == 63:
            for k3 in range(3):
                b.wait("gpsimd", s_wq2[k3], s_wq2[k3].n)
            s_cg.inc()
            b.emit("gpsimd", lambda e, h=s_cg.h: e.collective_compute(
                "AllGather", ALU.bypass, replica_groups=[[0, 1, 2, 3, 4, 5, 6, 7]],
                ins=[wqd_in.ap()], outs=[wqd_out.ap()]).then_inc(h, 1))
    vhs = {}
    tsq = {}
    str_prev = {}
    for fi in range(NF_G):
        r4 = fi % 2
        if fi >= 2:
            b.wait("tensor", s_v, vhs[fi - 2])
        ttr = s_t.inc()
        for c in range(NCH):
            b.emit("tensor", lambda e, c=c, fi=fi, r4=r4, last=(c == NCH - 1), h=s_t.h: (
                e.transpose(pb2bf[r4][:, 128 * c:128 * c + 128], in_=ytm[c][:, fi * 128:(fi + 1) * 128],
                            identity=identb[:]).then_inc(h, 1) if last else
                e.transpose(pb2bf[r4][:, 128 * c:128 * c + 128], in_=ytm[c][:, fi * 128:(fi + 1) * 128],
                            identity=identb[:])))
        if fi >= 1:
            b.wait("scalar", s_v, vhs[fi - 1])
        b.wait("scalar", s_gl[fi % 2], s_gl[fi % 2].n)
        ssil = s_s.inc()
        b.emit("scalar", lambda e, fi=fi, h=s_s.h: e.activation(out=silg[:], in_=gstg[fi % 2][:], func=AF.Silu).then_inc(h, 1))
        if fi + 2 < NF_G:
            b.wait("sync", s_s, ssil)
            b.dma("sync", s_gl[fi % 2], gstg[fi % 2][:], gate_dram[(fi + 2) * 128:(fi + 3) * 128, :])
        b.wait("vector", s_t, ttr)
        b.wait("vector", s_s, ssil)
        if fi >= 1:
            b.wait("vector", s_t, tsq[fi - 1])
        vv = s_v.inc()
        def hsq(e, fi=fi, r4=r4, h=s_v.h):
            e.drain()
            e.tensor_mul(out=gate_t[fi][:], in0=pb2bf[r4][:, 0:1024], in1=silg[:])
            e.drain()
            e.tensor_mul(out=sqt[:], in0=gate_t[fi][:], in1=gate_t[fi][:]).then_inc(h, 1)
        b.emit("vector", hsq)
        vhs[fi] = vv
        b.wait("tensor", s_v, vv)
        ts2 = s_t.inc()
        for ci, (c0, cw) in enumerate(TCO):
            b.emit("tensor", lambda e, c0=c0, cw=cw, fi=fi, ci=ci, last=(ci == 1), h=s_t.h: (
                e.matmul(pb[2 + ci][:], lhsT=ones_sb[:], rhs=sqt[:, c0:c0 + cw], start=(fi == 0), stop=(fi == NF_G - 1)).then_inc(h, 1) if last else
                e.matmul(pb[2 + ci][:], lhsT=ones_sb[:], rhs=sqt[:, c0:c0 + cw], start=(fi == 0), stop=(fi == NF_G - 1))))
        tsq[fi] = ts2
        emit_w2_step()
    b.wait("vector", s_t, s_t.n)
    inv_i = float(np.float32(1.0 / INTER))
    vst4 = s_v.inc()
    def gstat(e, h=s_v.h):
        e.tensor_scalar(out=rbcg[:, 0:512], in0=pb[2][:], scalar1=inv_i, scalar2=None, op0=ALU.mult)
        e.tensor_scalar(out=rbcg[:, 512:1024], in0=pb[3][:], scalar1=inv_i, scalar2=None, op0=ALU.mult).then_inc(h, 1)
    b.emit("vector", gstat)
    b.wait("scalar", s_v, vst4)
    sst4 = s_s.inc()
    def grstd(e, h=s_s.h):
        e.activation(out=rbcg[:], in_=rbcg[:], func=AF.Ln, bias=float(EPS), scale=1.0)
        e.drain()
        e.activation(out=rbcg[:], in_=rbcg[:], func=AF.Exp, bias=0.0, scale=-0.5).then_inc(h, 1)
    b.emit("scalar", grstd)
    b.wait("vector", s_s, sst4)
    b.wait("vector", s_ld, s_ld.n)
    vnrm = {}
    for fi in range(NF_G):
        vv = s_v.inc()
        def nrm(e, fi=fi, h=s_v.h):
            e.drain()
            e.scalar_tensor_tensor(out=gate_t[fi][:], in0=gate_t[fi][:], scalar=nwt[:, fi:fi + 1],
                                   in1=rbcg[:], op0=ALU.mult, op1=ALU.mult)
            e.drain().then_inc(h, 1)
        b.emit("vector", nrm)
        vnrm[fi] = vv
    s_wo = [Sem(nc, es, "wo0"), Sem(nc, es, "wo1")]
    s_x2 = Sem(nc, es, "x2")
    def wo_src(dtile, half):
        return w_out[half * 2048:(half + 1) * 2048, dtile * 128:(dtile + 1) * 128].rearrange("(k p) f -> p k f", p=128)
    b.dma("gpsimd", s_wo[0], panel_dst(0), wo_src(0, 0))
    b.dma("gpsimd", s_wo[1], panel_dst(1), wo_src(0, 1))
    pcnt4 = {0: 16, 1: 16}
    tres = {}
    vres = {}
    for dtile in range(ND):
        if dtile == 0:
            b.wait("sync", s_t, s_t.n)
        b.dma("sync", s_x2, resid[dtile], x_in[dtile * 128:(dtile + 1) * 128, 3:1027])
        r4 = dtile % 2
        pA, pB = pb[4 + 2 * r4], pb[5 + 2 * r4]
        for half in range(2):
            n4 = 2 * dtile + half
            j = n4 % 2
            b.wait("tensor", s_wo[j], pcnt4[n4 % 2] if False else pcnt4[j])
            if half == 0 and dtile == 0:
                b.wait("tensor", s_v, vnrm[NF_G - 1])
            if dtile >= 2 and half == 0:
                b.wait("tensor", s_v, vres[dtile - 2])
            ti = s_t.inc()
            for k in range(16):
                kk = half * 16 + k
                st = (half == 0 and k == 0); sp = (half == 1 and k == 15)
                b.emit("tensor", lambda e, j=j, k=k, kk=kk, st=st, sp=sp, pA=pA, pB=pB, last=(k == 15), h=s_t.h: (
                    (e.matmul(pA[:], lhsT=wpan[j][:, k * 128:(k + 1) * 128], rhs=gate_t[kk][:, 0:512], start=st, stop=sp),
                     e.matmul(pB[:], lhsT=wpan[j][:, k * 128:(k + 1) * 128], rhs=gate_t[kk][:, 512:1024], start=st, stop=sp).then_inc(h, 1))[-1] if last else
                    (e.matmul(pA[:], lhsT=wpan[j][:, k * 128:(k + 1) * 128], rhs=gate_t[kk][:, 0:512], start=st, stop=sp),
                     e.matmul(pB[:], lhsT=wpan[j][:, k * 128:(k + 1) * 128], rhs=gate_t[kk][:, 512:1024], start=st, stop=sp))[-1]))
            tres[n4] = ti
            nxt = n4 + 2
            if nxt < 2 * ND:
                b.wait("gpsimd", s_t, ti)
                b.dma("gpsimd", s_wo[j], panel_dst(j), wo_src(nxt // 2, nxt % 2))
                pcnt4[j] = s_wo[j].n
        b.wait("vector", s_t, tres[2 * dtile + 1])
        b.wait("vector", s_x2, s_x2.n)
        vv = s_v.inc()
        def racc(e, dtile=dtile, pA=pA, pB=pB, h=s_v.h):
            e.drain()
            e.tensor_add(out=resid[dtile][:, 0:512], in0=resid[dtile][:, 0:512], in1=pA[:])
            e.tensor_add(out=resid[dtile][:, 512:1024], in0=resid[dtile][:, 512:1024], in1=pB[:]).then_inc(h, 1)
        b.emit("vector", racc)
        vres[dtile] = vv
        b.wait("sync", s_v, vv)
        b.dma("sync", s_st, x1_dram[dtile * 128:(dtile + 1) * 128, :], resid[dtile])
        for _ in range(4):
            emit_w2_step()
    while w2_gi[0] < NW2:
        emit_w2_step()
    b.wait("sync", s_st, s_st.n)
    b.barrier()

    if "x1" in debug:
        dbx1 = b.dbg("x1dbg", [D, T])
        for i2 in range(ND):
            b.dma("sync", s_st, dbx1[i2 * 128:(i2 + 1) * 128, :], resid[i2])
        b.wait("sync", s_st, s_st.n)

    x2bf = [regE[:, j * 1024:(j + 1) * 1024] for j in range(2)]
    mbc2 = regEf[:, 1024:2048]
    rbc2 = regEf[:, 2048:3072]
    amr = regEf[:, 3072:4096]
    ln2t = regEf[:, 4096:5120]
    scalebc = regCf[:, 0:1024]
    rinvbc = regCf[:, 1024:2048]
    rmul = regCf[:, 2048:3072]
    xq_t = [regAB[:, 32768 + i2 * 1024: 32768 + (i2 + 1) * 1024] for i2 in range(ND)]
    tdone2 = {}
    for i2 in range(ND):
        j = i2 % 2
        if i2 >= 2:
            b.wait("vector", s_t, tdone2[i2 - 2])
        v1 = s_v.inc()
        b.emit("vector", lambda e, i2=i2, j=j, h=s_v.h: e.tensor_copy(out=x2bf[j][:], in_=resid[i2][:]).then_inc(h, 1))
        b.wait("tensor", s_v, v1)
        t1 = s_t.inc()
        for ci, (c0, cw) in enumerate(TCO):
            b.emit("tensor", lambda e, c0=c0, cw=cw, i2=i2, j=j, ci=ci, last=(ci == 1), h=s_t.h: (
                e.matmul(pb[ci][:], lhsT=ones_sb[:], rhs=x2bf[j][:, c0:c0 + cw], start=(i2 == 0), stop=(i2 == ND - 1)).then_inc(h, 1) if last else
                e.matmul(pb[ci][:], lhsT=ones_sb[:], rhs=x2bf[j][:, c0:c0 + cw], start=(i2 == 0), stop=(i2 == ND - 1))))
        b.wait("vector", s_t, t1)
        v2 = s_v.inc()
        b.emit("vector", lambda e, i2=i2, j=j, h=s_v.h: e.tensor_mul(out=x2bf[j][:], in0=resid[i2][:], in1=resid[i2][:]).then_inc(h, 1))
        b.wait("tensor", s_v, v2)
        t2 = s_t.inc()
        for ci, (c0, cw) in enumerate(TCO):
            b.emit("tensor", lambda e, c0=c0, cw=cw, i2=i2, j=j, ci=ci, last=(ci == 1), h=s_t.h: (
                e.matmul(pb[2 + ci][:], lhsT=ones_sb[:], rhs=x2bf[j][:, c0:c0 + cw], start=(i2 == 0), stop=(i2 == ND - 1)).then_inc(h, 1) if last else
                e.matmul(pb[2 + ci][:], lhsT=ones_sb[:], rhs=x2bf[j][:, c0:c0 + cw], start=(i2 == 0), stop=(i2 == ND - 1))))
        tdone2[i2] = t2
    b.wait("vector", s_t, s_t.n)
    vst5 = s_v.inc()
    def fin2(e, h=s_v.h):
        e.tensor_scalar(out=mbc2[:, 0:512], in0=pb[0][:], scalar1=inv_d, scalar2=None, op0=ALU.mult)
        e.tensor_scalar(out=mbc2[:, 512:1024], in0=pb[1][:], scalar1=inv_d, scalar2=None, op0=ALU.mult)
        e.tensor_scalar(out=rbc2[:, 0:512], in0=pb[2][:], scalar1=inv_d, scalar2=None, op0=ALU.mult)
        e.tensor_scalar(out=rbc2[:, 512:1024], in0=pb[3][:], scalar1=inv_d, scalar2=None, op0=ALU.mult)
        e.drain()
        e.tensor_mul(out=ln2t[:], in0=mbc2[:], in1=mbc2[:])
        e.drain()
        e.tensor_sub(out=rbc2[:], in0=rbc2[:], in1=ln2t[:]).then_inc(h, 1)
    b.emit("vector", fin2)
    b.wait("scalar", s_v, vst5)
    sst5 = s_s.inc()
    def rstd2(e, h=s_s.h):
        e.activation(out=rbc2[:], in_=rbc2[:], func=AF.Ln, bias=float(EPS), scale=1.0)
        e.drain()
        e.activation(out=rbc2[:], in_=rbc2[:], func=AF.Exp, bias=0.0, scale=-0.5).then_inc(h, 1)
    b.emit("scalar", rstd2)
    b.wait("vector", s_s, sst5)
    vam0 = s_v.inc()
    def am0(e, h=s_v.h):
        e.memset(amr[:], 0.0).then_inc(h, 1)
    b.emit("vector", am0)
    sam = {}
    for i2 in range(ND):
        vv = s_v.inc()
        def lnt(e, i2=i2, h=s_v.h):
            e.drain()
            e.tensor_sub(out=ln2t[:], in0=resid[i2][:], in1=mbc2[:])
            e.drain()
            e.tensor_mul(out=ln2t[:], in0=ln2t[:], in1=rbc2[:]).then_inc(h, 1)
        b.emit("vector", lnt)
        b.wait("scalar", s_v, vv)
        sv = s_s.inc()
        b.emit("scalar", lambda e, h=s_s.h: e.activation(out=regEf[:, 0:1024], in_=ln2t[:], func=AF.Abs).then_inc(h, 1))
        sam[i2] = sv
        b.wait("vector", s_s, sv)
        vv2 = s_v.inc()
        def amx(e, h=s_v.h):
            e.drain()
            e.tensor_max(out=amr[:], in0=amr[:], in1=regEf[:, 0:1024]).then_inc(h, 1)
        b.emit("vector", amx)
    amax_bcast(amr, scalebc)
    vrm = s_v.inc()
    def mkrm(e, h=s_v.h):
        e.tensor_scalar(out=scalebc[:], in0=scalebc[:], scalar1=float(EPS), scalar2=None, op0=ALU.max)
        e.drain()
        e.reciprocal(out=rmul[:], in_=scalebc[:])
        e.drain()
        e.tensor_scalar(out=scalebc[:], in0=rmul[:], scalar1=128.0, scalar2=None, op0=ALU.mult)
        e.drain()
        e.reciprocal(out=rinvbc[:], in_=scalebc[:])
        e.drain()
        e.tensor_mul(out=rmul[:], in0=rbc2[:], in1=scalebc[:]).then_inc(h, 1)
    b.emit("vector", mkrm)
    vq1 = {}
    for i2 in range(ND):
        vv = s_v.inc()
        def q1(e, i2=i2, h=s_v.h):
            e.drain()
            e.tensor_sub(out=ln2t[:], in0=resid[i2][:], in1=mbc2[:])
            e.drain()
            e.tensor_mul(out=ln2t[:], in0=ln2t[:], in1=rmul[:])
            e.drain()
            e.tensor_scalar(out=ln2t[:], in0=ln2t[:], scalar1=RC, scalar2=RC, op0=ALU.add, op1=ALU.subtract)
            e.drain()
            e.tensor_scalar(out=xq_t[i2][:], in0=ln2t[:], scalar1=127.0, scalar2=-128.0, op0=ALU.min, op1=ALU.max).then_inc(h, 1)
        b.emit("vector", q1)
        vq1[i2] = vv
    b.barrier()

    if "xq" in debug:
        dbxq = b.dbg("xqdbg", [D, T])
        for i2 in range(ND):
            b.dma("gpsimd", s_gst, dbxq[i2 * 128:(i2 + 1) * 128, :], xq_t[i2][:])
        b.wait("gpsimd", s_gst, s_gst.n)

    b.dma("sync", s_ld, H128_sb[:], H128_c.ap())
    b.dma("sync", s_ld, H64b_sb[:], H64b_c.ap())
    b.wait("sync", s_ld, s_ld.n)
    b.wait("gpsimd", s_cg, s_cg.n)
    b.barrier()
    sbcg = regEf[:, 0:1024]
    gv2 = [regEf[:, 1024:2048], regEf[:, 3072:4096]]
    intert2 = [regE[:, 4096:5120], regE[:, 8192:9216]]
    ztile2 = [regE[:, 5120:6144], regE[:, 9216:10240]]
    s_zs = [Sem(nc, es, "zs0"), Sem(nc, es, "zs1")]
    ssil6 = {}
    tz66 = {}
    vpre = s_v.inc()
    def mksbcg(e, h=s_v.h):
        e.tensor_scalar(out=sbcg[:], in0=rinvbc[:], scalar1=wqrow[:, 4:5], scalar2=None, op0=ALU.mult).then_inc(h, 1)
    b.emit("vector", mksbcg)
    s_wp6g = [Sem(nc, es, "wp6g0"), Sem(nc, es, "wp6g1")]
    s_wp6u = [Sem(nc, es, "wp6u0"), Sem(nc, es, "wp6u1")]
    gpan6 = [wpan[0][:], regC[:, 4096:6144]]
    upan6 = [wpan[1][:], regC[:, 6144:8192]]
    def pan6_dst(buf):
        return buf.rearrange("p (k f) -> p k f", f=128)
    def load_gu_panel(sem, buf, wout, ot):
        r = ot // 8
        c0 = (ot % 8) * 128
        dst = pan6_dst(buf)
        for p in range(4):
            src = wout.ap()[p * 4096 + r * 512:p * 4096 + (r + 1) * 512, c0:c0 + 128].rearrange("(k q) f -> q k f", q=128)
            b.dma("sync", sem, dst[:, 4 * p:4 * p + 4, :], src)
    load_gu_panel(s_wp6g[0], gpan6[0], wqg_out, 0)
    load_gu_panel(s_wp6u[0], upan6[0], wqu_out, 0)
    NOT = FFI // 128
    tg6 = {}
    vz6 = {}
    sz6 = {}
    for ot in range(NOT):
        pi = ot % 2
        pgA, pgB = pb[0 + 2 * pi], pb[1 + 2 * pi]
        puA, puB = pb[4 + 2 * pi], pb[5 + 2 * pi]
        if ot + 1 < NOT:
            if ot >= 1:
                b.wait("sync", s_t, tg6[ot - 1])
            jn = (ot + 1) % 2
            load_gu_panel(s_wp6g[jn], gpan6[jn], wqg_out, ot + 1)
            load_gu_panel(s_wp6u[jn], upan6[jn], wqu_out, ot + 1)
        b.wait("tensor", s_wp6g[pi], s_wp6g[pi].n - (64 if (ot + 1 < NOT and (ot + 1) % 2 == pi) else 0))
        b.wait("tensor", s_wp6u[pi], s_wp6u[pi].n - (64 if (ot + 1 < NOT and (ot + 1) % 2 == pi) else 0))
        if ot == 0:
            b.wait("tensor", s_v, vpre)
        if ot >= 2:
            b.wait("tensor", s_s, sz6[ot - 2])
            b.wait("tensor", s_v, vz6[ot - 2])
        ti = s_t.inc()
        for k in range(ND):
            st = (k == 0); sp = (k == ND - 1)
            b.emit("tensor", lambda e, k=k, st=st, sp=sp, pgA=pgA, pgB=pgB, gp=gpan6[pi]: (
                e.matmul(pgA[:], lhsT=gp[:, k * 128:(k + 1) * 128], rhs=xq_t[k][:, 0:512], start=st, stop=sp),
                e.matmul(pgB[:], lhsT=gp[:, k * 128:(k + 1) * 128], rhs=xq_t[k][:, 512:1024], start=st, stop=sp))[-1] and None)
        for k in range(ND):
            st = (k == 0); sp = (k == ND - 1)
            b.emit("tensor", lambda e, k=k, st=st, sp=sp, puA=puA, puB=puB, up=upan6[pi], last=(k == ND - 1), h=s_t.h: (
                (e.matmul(puA[:], lhsT=up[:, k * 128:(k + 1) * 128], rhs=xq_t[k][:, 0:512], start=st, stop=sp),
                 e.matmul(puB[:], lhsT=up[:, k * 128:(k + 1) * 128], rhs=xq_t[k][:, 512:1024], start=st, stop=sp).then_inc(h, 1))[-1] if last else
                (e.matmul(puA[:], lhsT=up[:, k * 128:(k + 1) * 128], rhs=xq_t[k][:, 0:512], start=st, stop=sp),
                 e.matmul(puB[:], lhsT=up[:, k * 128:(k + 1) * 128], rhs=xq_t[k][:, 512:1024], start=st, stop=sp))[-1] and None))
        tg6[ot] = ti
        j6 = ot % 2
        b.wait("vector", s_t, ti)
        if ot >= 2:
            b.wait("vector", s_s, ssil6[ot - 2])
        vv = s_v.inc()
        def ep6(e, j6=j6, pgA=pgA, pgB=pgB, h=s_v.h):
            e.drain()
            e.tensor_mul(out=gv2[j6][:, 0:512], in0=pgA[:], in1=sbcg[:, 0:512])
            e.tensor_mul(out=gv2[j6][:, 512:1024], in0=pgB[:], in1=sbcg[:, 512:1024]).then_inc(h, 1)
        b.emit("vector", ep6)
        b.wait("scalar", s_v, vv)
        sv = s_s.inc()
        b.emit("scalar", lambda e, j6=j6, h=s_s.h: e.activation(out=gv2[j6][:], in_=gv2[j6][:], func=AF.Silu).then_inc(h, 1))
        ssil6[ot] = sv
        b.wait("vector", s_s, sv)
        if ot >= 2:
            b.wait("vector", s_t, tz66[ot - 2])
        vv2 = s_v.inc()
        def ep6b(e, j6=j6, puA=puA, puB=puB, h=s_v.h):
            e.drain()
            e.tensor_mul(out=intert2[j6][:, 0:512], in0=gv2[j6][:, 0:512], in1=puA[:])
            e.tensor_mul(out=intert2[j6][:, 512:1024], in0=gv2[j6][:, 512:1024], in1=puB[:]).then_inc(h, 1)
        b.emit("vector", ep6b)
        vz6[ot] = vv2
        b.wait("tensor", s_v, vv2)
        tz = s_t.inc()
        def st1(e, j6=j6, pgA=pgA, pgB=pgB, h=s_t.h):
            e.matmul(pgA[:], lhsT=H128_sb[:], rhs=intert2[j6][:, 0:512], start=True, stop=True)
            e.matmul(pgB[:], lhsT=H128_sb[:], rhs=intert2[j6][:, 512:1024], start=True, stop=True).then_inc(h, 1)
        b.emit("tensor", st1)
        tz66[ot] = tz
        b.wait("scalar", s_t, tz)
        b.wait("scalar", s_zs[j6], s_zs[j6].n)
        sz = s_s.inc()
        def ze(e, j6=j6, pgA=pgA, pgB=pgB, h=s_s.h):
            e.copy(out=ztile2[j6][:, 0:512], in_=pgA[:])
            e.copy(out=ztile2[j6][:, 512:1024], in_=pgB[:]).then_inc(h, 1)
        b.emit("scalar", ze)
        sz6[ot] = sz
        b.wait("sync", s_s, sz)
        b.dma("sync", s_zs[j6], zdram[ot * 128:(ot + 1) * 128, :], ztile2[j6][:])
    b.wait("sync", s_zs[0], s_zs[0].n)
    b.wait("sync", s_zs[1], s_zs[1].n)
    b.barrier()

    vtile = [regE[:, 0:1024], regE[:, 1024:2048], regC[:, 8192:9216], regC[:, 9216:10240],
             regC[:, 10240:11264], regC[:, 11264:12288]]
    abstl = regEf[:, 1024:2048]
    amr2 = regEf[:, 2048:3072]
    qmulbc = regEf[:, 3072:4096]
    rfinbc = regEf[:, 4096:5120]
    W_t = [regAB[:, m * 1024:(m + 1) * 1024] for m in range(64)]
    s_v6 = [Sem(nc, es, "v6_%d" % j) for j in range(6)]
    zdr = zdram.rearrange("(i j) t -> i j t", j=128)
    tW = {}
    sW = {}
    vA = {}
    for m in range(64):
        j = m % 6
        pi = m % 4
        pwA, pwB = pb[2 * pi], pb[2 * pi + 1]
        if m >= 6:
            b.wait("sync", s_t, tW[m - 6])
        for jr in range(2):
            for hh in range(2):
                b.dma("sync", s_v6[j], vtile[j][jr * 64 + hh * 32:jr * 64 + (hh + 1) * 32, :],
                      zdr[hh * 32:(hh + 1) * 32, 2 * m + jr, :])
        b.wait("tensor", s_v6[j], s_v6[j].n)
        if m >= 4:
            b.wait("tensor", s_v, vA[m - 4])
        ti = s_t.inc()
        def st2f(e, j=j, pwA=pwA, pwB=pwB, h=s_t.h):
            e.matmul(pwA[:], lhsT=H64b_sb[:], rhs=vtile[j][:, 0:512], start=True, stop=True)
            e.matmul(pwB[:], lhsT=H64b_sb[:], rhs=vtile[j][:, 512:1024], start=True, stop=True).then_inc(h, 1)
        b.emit("tensor", st2f)
        tW[m] = ti
        b.wait("scalar", s_t, ti)
        sv = s_s.inc()
        def evW(e, m=m, pwA=pwA, pwB=pwB, h=s_s.h):
            e.copy(out=W_t[m][:, 0:512], in_=pwA[:])
            e.copy(out=W_t[m][:, 512:1024], in_=pwB[:]).then_inc(h, 1)
        b.emit("scalar", evW)
        sW[m] = sv
        b.wait("vector", s_s, sv)
        vv = s_v.inc()
        def mmx(e, m=m, pwA=pwA, pwB=pwB, h=s_v.h):
            e.drain()
            if m == 0:
                e.tensor_copy(out=amr2[:, 0:512], in_=pwA[:])
                e.tensor_copy(out=amr2[:, 512:1024], in_=pwB[:])
                e.tensor_copy(out=abstl[:, 0:512], in_=pwA[:])
                e.tensor_copy(out=abstl[:, 512:1024], in_=pwB[:]).then_inc(h, 1)
            else:
                e.tensor_max(out=amr2[:, 0:512], in0=amr2[:, 0:512], in1=pwA[:])
                e.tensor_max(out=amr2[:, 512:1024], in0=amr2[:, 512:1024], in1=pwB[:])
                e.tensor_tensor(out=abstl[:, 0:512], in0=abstl[:, 0:512], in1=pwA[:], op=ALU.min)
                e.tensor_tensor(out=abstl[:, 512:1024], in0=abstl[:, 512:1024], in1=pwB[:], op=ALU.min).then_inc(h, 1)
        b.emit("vector", mmx)
        vA[m] = vv
    SQN = float(np.sqrt(FFI))
    vcmb = s_v.inc()
    def cmb6(e, h=s_v.h):
        e.drain()
        e.tensor_scalar(out=abstl[:], in0=abstl[:], scalar1=-1.0, scalar2=None, op0=ALU.mult)
        e.drain()
        e.tensor_max(out=amr2[:], in0=amr2[:], in1=abstl[:]).then_inc(h, 1)
    b.emit("vector", cmb6)
    amax_bcast(amr2, abstl)
    vtr2 = s_v.inc()
    def rows6(e, h=s_v.h):
        e.tensor_mul(out=abstl[:], in0=abstl[:], in1=rinvbc[:])
        e.drain()
        e.tensor_scalar(out=abstl[:], in0=abstl[:], scalar1=wqrow[:, 5:6], scalar2=None, op0=ALU.mult)
        e.drain()
        e.tensor_scalar(out=abstl[:], in0=abstl[:], scalar1=float(1.0 / SQN), scalar2=float(EPS), op0=ALU.mult, op1=ALU.max)
        e.drain()
        e.reciprocal(out=amr2[:], in_=abstl[:])
        e.drain()
        e.tensor_scalar(out=amr2[:], in0=amr2[:], scalar1=128.0, scalar2=None, op0=ALU.mult)
        e.drain()
        e.tensor_mul(out=qmulbc[:], in0=amr2[:], in1=rinvbc[:])
        e.drain()
        e.tensor_scalar(out=qmulbc[:], in0=qmulbc[:], scalar1=wqrow[:, 5:6], scalar2=float(1.0 / SQN), op0=ALU.mult, op1=ALU.mult)
        e.drain()
        e.reciprocal(out=rfinbc[:], in_=amr2[:])
        e.drain()
        e.tensor_scalar(out=rfinbc[:], in0=rfinbc[:], scalar1=wqrow[:, 6:7], scalar2=None, op0=ALU.mult).then_inc(h, 1)
    b.emit("vector", rows6)
    b.wait("vector", s_v, vtr2)
    b.wait("gpsimd", s_v, vtr2)
    vq2 = {}
    srq = {}
    for m in range(64):
        scr = abstl if m % 2 == 0 else amr2
        if m >= 2:
            b.wait("vector", s_s, srq[m - 2])
        vv1 = s_v.inc()
        def q2a(e, m=m, scr=scr, h=s_v.h):
            e.drain()
            e.tensor_mul(out=scr[:], in0=W_t[m][:], in1=qmulbc[:]).then_inc(h, 1)
        b.emit("vector", q2a)
        b.wait("scalar", s_v, vv1)
        sr = s_s.inc()
        def q2b(e, scr=scr, h=s_s.h):
            e.activation(out=scr[:], in_=scr[:], func=AF.Identity, bias=RC)
            e.drain()
            e.activation(out=scr[:], in_=scr[:], func=AF.Identity, bias=-RC).then_inc(h, 1)
        b.emit("scalar", q2b)
        srq[m] = sr
        b.wait("vector", s_s, sr)
        vv2 = s_v.inc()
        def q2c(e, m=m, scr=scr, h=s_v.h):
            e.drain()
            e.tensor_scalar(out=W_t[m][:], in0=scr[:], scalar1=127.0, scalar2=-128.0, op0=ALU.min, op1=ALU.max).then_inc(h, 1)
        b.emit("vector", q2c)
        vq2[m] = ("vector", vv2)
    b.barrier()

    dpan = [regC[:, 4096 + j * 4096: 8192 + j * 4096] for j in range(2)]
    outt = [regCf[:, 6144 + j * 1024: 7168 + j * 1024] for j in range(2)]
    x1t = [regEf[:, j * 1024:(j + 1) * 1024] for j in range(2)]
    s_dp = [Sem(nc, es, "dp0"), Sem(nc, es, "dp1")]
    s_x6 = Sem(nc, es, "x6")
    s_yo = [Sem(nc, es, "yo0"), Sem(nc, es, "yo1")]
    wdr = wqd_out.ap()
    def dpan_src(ot2, half):
        r = ot2 // 2
        c0 = (ot2 % 2) * 128
        blk = wdr[r * FFI:(r + 1) * FFI, c0:c0 + 128]
        return blk.rearrange("(i j) o -> i j o", j=128)[:, 64 * half:64 * half + 64, :].transpose([1, 0, 2]).rearrange("j i o -> ???") if False else blk
    def dpan_src2(ot2, half):
        r = ot2 // 2
        c0 = (ot2 % 2) * 128
        blk = wdr[r * FFI:(r + 1) * FFI, c0:c0 + 128]
        v = blk.rearrange("(i m b) o -> b i m o", b=2, m=64)
        return v[:, :, 32 * half:32 * half + 32, :]
    def dpan_dst(j):
        return wpanD[j].rearrange("p (m o) -> p m o", o=128)
    wpanD = dpan
    tdn = {}
    vdn = {}
    def dpan_load(j, ot2, half):
        v = dpan_src2(ot2, half)
        dst = wpanD[j].rearrange("p (m o) -> p m o", o=128)
        b.dma("sync", s_dp[j], dst[0:64, :, :], v[0, :, :, :])
        b.dma("sync", s_dp[j], dst[64:128, :, :], v[1, :, :, :])
    dpan_load(0, 0, 0)
    dpan_load(1, 0, 1)
    pcntd = {0: 32, 1: 32}
    for ot2 in range(ND):
        pi = ot2 % 2
        pdA, pdB = pb[4 + 2 * pi], pb[5 + 2 * pi]
        b.dma("sync", s_x6, x1t[pi][:], x1_dram[ot2 * 128:(ot2 + 1) * 128, :])
        for half in range(2):
            n6 = 2 * ot2 + half
            j = n6 % 2
            b.wait("tensor", s_dp[j], pcntd[j])
            if ot2 >= 2 and half == 0:
                b.wait("tensor", s_v, vdn[ot2 - 2])
            ti = s_t.inc()
            for mi in range(32):
                m = half * 32 + mi
                st = (half == 0 and mi == 0); sp = (half == 1 and mi == 31)
                b.emit("tensor", lambda e, j=j, mi=mi, m=m, st=st, sp=sp, pdA=pdA, pdB=pdB, last=(mi == 31), h=s_t.h: (
                    (e.matmul(pdA[:], lhsT=wpanD[j][:, mi * 128:(mi + 1) * 128], rhs=W_t[m][:, 0:512], start=st, stop=sp),
                     e.matmul(pdB[:], lhsT=wpanD[j][:, mi * 128:(mi + 1) * 128], rhs=W_t[m][:, 512:1024], start=st, stop=sp).then_inc(h, 1))[-1] if last else
                    (e.matmul(pdA[:], lhsT=wpanD[j][:, mi * 128:(mi + 1) * 128], rhs=W_t[m][:, 0:512], start=st, stop=sp),
                     e.matmul(pdB[:], lhsT=wpanD[j][:, mi * 128:(mi + 1) * 128], rhs=W_t[m][:, 512:1024], start=st, stop=sp))[-1] and None))
            tdn[n6] = ti
            nxt = n6 + 2
            if nxt < 2 * ND:
                b.wait("sync", s_t, ti)
                dpan_load(j, nxt // 2, nxt % 2)
                pcntd[j] = s_dp[j].n
        b.wait("vector", s_t, tdn[2 * ot2 + 1])
        b.wait("vector", s_x6, s_x6.n)
        if ot2 >= 2:
            b.wait("vector", s_yo[pi], s_yo[pi].n)
        vv = s_v.inc()
        def dne(e, ot2=ot2, pi=pi, pdA=pdA, pdB=pdB, h=s_v.h):
            e.tensor_mul(out=outt[pi][:, 0:512], in0=pdA[:], in1=rfinbc[:, 0:512])
            e.tensor_mul(out=outt[pi][:, 512:1024], in0=pdB[:], in1=rfinbc[:, 512:1024])
            e.drain()
            e.tensor_add(out=outt[pi][:], in0=outt[pi][:], in1=x1t[pi][:]).then_inc(h, 1)
        b.emit("vector", dne)
        vdn[ot2] = vv
        b.wait("sync", s_v, vv)
        b.dma("sync", s_yo[pi], y_out[ot2 * 128:(ot2 + 1) * 128, :], outt[pi][:])
    b.wait("sync", s_yo[0], s_yo[0].n)
    b.wait("sync", s_yo[1], s_yo[1].n)
    b.wait("tensor", s_yo[0], s_yo[0].n)
    b.wait("tensor", s_yo[1], s_yo[1].n)

    if "xc" in debug:
        dbgx = b.dbg("xc", [INTER, T])
        for c in range(NCH):
            for i in range(NF_G):
                b.dma("gpsimd", s_gst, dbgx[i * 128:(i + 1) * 128, c * 128:(c + 1) * 128], xpr[:, c, i, :])
        dbgB = b.dbg("Bdbg", [128, T]); dbgC = b.dbg("Cdbg", [128, T]); dbgdt = b.dbg("dtdbg", [64, TH])
        b.dma("gpsimd", s_gst, dbgB, Bc[:])
        b.dma("gpsimd", s_gst, dbgC, Cc[:])
        b.dma("gpsimd", s_gst, dbgdt, dt32[:])
        b.wait("gpsimd", s_gst, s_gst.n)

    return b


def _stage(inputs):
    x = inputs["L0_attn_hidden_states"]
    cw = np.ascontiguousarray(inputs["L0_attn_conv_w"].reshape(NF_XBC, 128, 4).transpose(1, 0, 2).reshape(128, NF_XBC * 4))
    cb = np.ascontiguousarray(inputs["L0_attn_conv_b"].reshape(NF_XBC, 128).T)
    nw = np.ascontiguousarray(inputs["L0_attn_norm_w"].reshape(NF_G, 128).T)
    in_maps = []
    for c in range(NCORES):
        bi, hf = c // 2, c % 2
        t0 = hf * T
        xs = np.zeros((TH, D), np.float32)
        xs[3:] = x[bi, t0:t0 + T]
        if hf == 1:
            xs[:3] = x[bi, t0 - 3:t0]
        m = {
            "x_in": np.ascontiguousarray(xs.T),
            "w_xbc": inputs["L0_attn_W_xBC"],
            "w_z": inputs["L0_attn_W_z"],
            "w_dt": inputs["L0_attn_W_dt"],
            "w_out": inputs["L0_attn_W_out"],
            "cw_in": cw, "cb_in": cb, "nw_in": nw,
            "dtb_in": inputs["L0_attn_dt_bias"].reshape(64, 1),
            "alog_in": inputs["L0_attn_A_log"].reshape(64, 1),
            "dvec_in": inputs["L0_attn_D"].reshape(64, 1),
            "wg_sh": np.ascontiguousarray(inputs["L0_ffn_w_gate"][c * T:(c + 1) * T].T),
            "wu_sh": np.ascontiguousarray(inputs["L0_ffn_w_up"][c * T:(c + 1) * T].T),
            "wd_sh": np.ascontiguousarray(inputs["L0_ffn_w_down"][c * 256:(c + 1) * 256].T),
            "smask": np.full((128, 1), float(hf), np.float32),
        }
        in_maps.append(m)
    return in_maps


def kernel(**inputs):
    in_maps = _stage(inputs)
    b = build()
    nc = b.finish()
    res = run_bass_kernel_spmd(nc, in_maps, core_ids=list(range(NCORES)))
    out = np.zeros((4, 2048, 2048), np.float32)
    for c in range(NCORES):
        bi, hf = c // 2, c % 2
        out[bi, hf * T:(hf + 1) * T, :] = res.results[c]["y_out"].T
    return out


# version 26
# speedup vs baseline: 1.0321x; 1.0321x over previous
import numpy as np
import ml_dtypes
from contextlib import ExitStack
import concourse.bass as bass
import concourse.mybir as mybir
from concourse.bass_utils import run_bass_kernel_spmd


F32 = mybir.dt.float32
BF16 = mybir.dt.bfloat16
F32R = mybir.dt.float32r
AF = mybir.ActivationFunctionType
ALU = mybir.AluOpType

NCORES = 8
T = 1024
TH = T + 3
D = 2048
INTER = 4096
HEADS = 64
P = 64
N = 128
CONV = INTER + 2 * N
NF_XBC = CONV // 128
NF_G = INTER // 128
ND = D // 128
Q = 128
NCH = T // Q
FFI = 8192
EPS = 1e-5
RC = float(np.float32(1.5 * 2 ** 23))
SHARED_AG = True

TCH = [(0, 512), (512, 512), (1024, 3)]
TCO = [(0, 512), (512, 512)]


class Sem:
    def __init__(self, nc, es, name):
        self.h = es.enter_context(nc.semaphore(name))
        self.n = 0
    def inc(self, k=1):
        self.n += k
        return self.n


class Builder:
    def __init__(self, debug=()):
        self.debug = set(debug)
        self.nc = bass.Bass("TRN2", target_bir_lowering=False)
        self.es = ExitStack()
        self.ops = {k: [] for k in ("tensor", "vector", "scalar", "gpsimd", "sync")}
        self._consts = {}
        self._nbar = 0

    def emit(self, eng, fn):
        self.ops[eng].append(fn)

    def barrier(self):
        s = Sem(self.nc, self.es, f"bar{self._nbar}")
        self._nbar += 1
        engs = ("tensor", "vector", "scalar", "gpsimd", "sync")
        for e in engs:
            s.inc(1)
            self.emit(e, lambda x, h=s.h: x.drain().then_inc(h, 1))
        tot = s.n
        for e in engs:
            self.emit(e, lambda x, h=s.h, tot=tot: x.wait_ge(h, tot))

    def wait(self, eng, sem, val=None):
        v = sem.n if val is None else val
        assert v <= sem.n, "waiting on a future semaphore value"
        self.emit(eng, lambda e, h=sem.h, v=v: e.wait_ge(h, v))

    def dma(self, eng, sem, out, in_, **kw):
        v = sem.inc(16)
        self.emit(eng, lambda e, o=out, i=in_, h=sem.h, kw=kw: e.dma_start(out=o, in_=i, **kw).then_inc(h, 16))
        return v

    def sb(self, name, shape, dtype, off):
        return self.nc.alloc_sbuf_tensor_at(name, list(shape), dtype, offset=off)

    def const(self, name, arr):
        if name not in self._consts:
            self._consts[name] = self.nc.inline_tensor(np.ascontiguousarray(arr), name)
        return self._consts[name]

    def dbg(self, name, shape, dtype=F32):
        return self.nc.dram_tensor("dbg_" + name, list(shape), dtype, kind="ExternalOutput").ap()

    def finish(self):
        nc = self.nc
        with nc.Block() as block:
            @block.tensor
            def _(e):
                for fn in self.ops["tensor"]: fn(e)
            @block.vector
            def _(e):
                for fn in self.ops["vector"]: fn(e)
            @block.scalar
            def _(e):
                for fn in self.ops["scalar"]: fn(e)
            @block.gpsimd
            def _(e):
                for fn in self.ops["gpsimd"]: fn(e)
            @block.sync
            def _(e):
                for fn in self.ops["sync"]: fn(e)
        self.es.close()
        return nc


def build(debug=()):
    b = Builder(debug)
    nc = b.nc
    es = b.es

    dram_in = lambda n, s: nc.dram_tensor(n, list(s), F32, kind="ExternalInput").ap()
    x_in = dram_in("x_in", [D, TH])
    w_xbc = dram_in("w_xbc", [D, CONV])
    w_z = dram_in("w_z", [D, INTER])
    w_dt = dram_in("w_dt", [D, HEADS])
    w_out = dram_in("w_out", [INTER, D])
    cw_in = dram_in("cw_in", [128, NF_XBC * 4])
    cb_in = dram_in("cb_in", [128, NF_XBC])
    dtb_in = dram_in("dtb_in", [64, 1])
    alog_in = dram_in("alog_in", [64, 1])
    dvec_in = dram_in("dvec_in", [64, 1])
    nw_in = dram_in("nw_in", [128, NF_G])
    wg_sh = dram_in("wg_sh", [D, T])
    wu_sh = dram_in("wu_sh", [D, T])
    wd_sh = dram_in("wd_sh", [FFI, 256])
    smask = dram_in("smask", [128, 1])
    y_out = nc.dram_tensor("y_out", [D, T], F32, kind="ExternalOutput").ap()
    x1_dram = nc.dram_tensor("x1_scratch", [D, T], F32, kind="Internal").ap()
    gate_dram = nc.dram_tensor("gate_scratch", [INTER, T], BF16, kind="Internal").ap()
    zdram = nc.dram_tensor("z_scratch", [FFI, T], BF16, kind="Internal").ap()
    wqg_in = nc.dram_tensor("wqg_in", [D, T], BF16, kind="Internal")
    wqu_in = nc.dram_tensor("wqu_in", [D, T], BF16, kind="Internal")
    wqd_in = nc.dram_tensor("wqd_in", [FFI, 256], BF16, kind="Internal")
    asp = "Shared" if SHARED_AG else "Local"
    wqg_out = nc.dram_tensor("wqg_out", [8 * D, T], BF16, kind="Internal", addr_space=asp)
    wqu_out = nc.dram_tensor("wqu_out", [8 * D, T], BF16, kind="Internal", addr_space=asp)
    wqd_out = nc.dram_tensor("wqd_out", [8 * FFI, 256], BF16, kind="Internal", addr_space=asp)
    ar_in = nc.dram_tensor("ar_in", [1, 4], F32, kind="Internal")
    ar_out = nc.dram_tensor("ar_out", [1, 4], F32, kind="Internal")

    s_ld = Sem(nc, es, "ld")
    s_wp = Sem(nc, es, "wp")
    s_t = Sem(nc, es, "t")
    s_v = Sem(nc, es, "v")
    s_s = Sem(nc, es, "s")
    s_st = Sem(nc, es, "st")
    s_gst = Sem(nc, es, "gst")

    SB0 = 16512
    A0, B0, C0, D0, E0 = SB0, SB0 + 65536, SB0 + 131072, SB0 + 164352, SB0 + 180224
    regAB_h = b.sb("regAB", [128, 65536], BF16, A0)
    regC_h = b.sb("regC", [128, 16640], BF16, C0)
    regE_h = b.sb("regE", [128, 10320], BF16, E0)
    regAB = regAB_h.ap(); regABf = regAB.bitcast(F32)
    regC = regC_h.ap(); regCf = regC.bitcast(F32)
    regE = regE_h.ap(); regEf = regE.bitcast(F32)
    xtmp = [regABf[:, i * 1032: i * 1032 + TH] for i in range(ND)]
    xpart = regAB[:, 0:NCH * NF_G * Q]
    lnx1 = [regC[:, i * 1040: i * 1040 + TH] for i in range(ND)]
    Bc = b.sb("Bc", [128, T], BF16, D0)
    Cc = b.sb("Cc", [128, T], BF16, D0 + 2048)
    dt32 = b.sb("dt32", [64, TH], F32, D0 + 4096)
    cwt = b.sb("cwt", [128, NF_XBC * 4], F32, D0 + 8224)
    cbt = b.sb("cbt", [128, NF_XBC], F32, D0 + 8768)
    abt = b.sb("abt", [64, 1], F32, D0 + 8928)
    dbt = b.sb("dbt", [64, 1], F32, D0 + 8960)
    dtbt = b.sb("dtbt", [64, 1], F32, D0 + 8992)
    abtA = b.sb("abtA", [64, 1], F32, D0 + 9024)
    mbc = regEf[:, 0:TH]
    rbc = regEf[:, 1032:1032 + TH]
    cv0 = regEf[:, 2064:2064 + TH]
    cv1 = regEf[:, 3096:3096 + T]
    xbf = [regE[:, 8240 + j * 1040: 8240 + j * 1040 + TH] for j in range(2)]
    wpan = [b.sb(f"wpan{j}", [128, 2048], BF16, E0 + 20640 + j * 4096) for j in range(2)]
    ones_sb = b.sb("ones_sb", [128, 128], BF16, E0 + 28832)
    ctile = b.sb("ctile", [128, 8], F32, E0 + 29088)
    identf = b.sb("identf", [128, 128], F32, E0 + 29152)
    sel127 = b.sb("sel127", [128, 128], F32, E0 + 29664)
    triu_m = b.sb("triu_m", [128, Q], F32, E0 + 30176)
    ones1f = b.sb("ones1f", [33, 128], F32, E0 + 30688)
    identb = b.sb("identb", [128, 128], BF16, E0 + 31200)
    Dbc = b.sb("Dbc", [128, 64], F32, E0 + 31456)
    dvr = b.sb("dvr", [1, 64], F32, E0 + 31712)
    smt = b.sb("smt", [128, 1], F32, E0 + 31968)
    onescol = b.sb("onescol", [128, 1], F32, E0 + 32000)
    H128_sb = b.sb("H128_sb", [128, 128], BF16, E0 + 32032)
    H64b_sb = b.sb("H64b_sb", [128, 128], BF16, E0 + 32288)
    nwt = b.sb("nwt", [128, NF_G], F32, D0 + 9056)
    mrow = b.sb("mrow", [1, 16], F32, D0 + 9184)
    wqrow = b.sb("wqrow", [128, 8], F32, D0 + 9248)
    amtm = b.sb("amtm", [128, 8], F32, D0 + 9280)
    coltl = b.sb("coltl", [128, 128], F32, D0 + 9312)

    CVALS = [float(EPS), 1.0, -0.5, RC, -RC, 10.0, 2.0, 0.5]
    for ci, v in enumerate(CVALS):
        nc.const_aps.register(F32, v, ctile[:, ci:ci + 1])
    cvals_c = b.const("cvals_c", np.tile(np.array(CVALS, np.float32), (128, 1)))

    pb = [es.enter_context(nc.psum_tensor(f"pb{i}", [128, 512], F32)) for i in range(8)]

    ones_c = b.const("ones_c", np.ones((128, 128), dtype=ml_dtypes.bfloat16))
    def _had(n):
        h = np.array([[1.0]], np.float32)
        while h.shape[0] < n:
            h = np.block([[h, h], [h, -h]])
        return h
    H128_c = b.const("H128_c", _had(128).astype(ml_dtypes.bfloat16))
    _h64 = _had(64)
    _hb = np.zeros((128, 128), np.float32)
    _hb[0:64, 0:64] = _h64
    _hb[64:128, 64:128] = _h64
    H64b_c = b.const("H64b_c", _hb.astype(ml_dtypes.bfloat16))


    b.dma("sync", s_ld, ctile[:], cvals_c.ap())
    b.dma("sync", s_ld, ones_sb[:], ones_c.ap())
    onescol_c = b.const("onescol_c", np.ones((128, 1), np.float32))
    b.dma("sync", s_ld, onescol[:], onescol_c.ap())
    identf_c = b.const("identf_c", np.eye(128, dtype=np.float32))
    identb_c = b.const("identb_c", np.eye(128).astype(ml_dtypes.bfloat16))
    _o1 = np.zeros((33, 128), np.float32); _o1[0] = 1.0; _o1[32] = 1.0
    ones1f_c = b.const("ones1f_c", _o1)
    s127_c = b.const("s127_c", (np.arange(128)[:, None] == 127).astype(np.float32) * np.ones((1, 128), np.float32))
    triu_c = b.const("triu_c", np.triu(np.ones((128, 128), np.float32)))
    b.dma("sync", s_ld, identf[:], identf_c.ap())
    b.dma("sync", s_ld, identb[:], identb_c.ap())
    b.dma("sync", s_ld, ones1f[:], ones1f_c.ap())
    b.dma("sync", s_ld, sel127[:], s127_c.ap())
    b.dma("sync", s_ld, triu_m[:], triu_c.ap())
    b.dma("sync", s_ld, dvr[:], dvec_in.transpose([1, 0]))
    b.dma("sync", s_ld, cwt[:], cw_in)
    b.dma("sync", s_ld, cbt[:], cb_in)
    b.dma("sync", s_ld, abt[:], alog_in)
    b.dma("sync", s_ld, dbt[:], dvec_in)
    b.dma("sync", s_ld, dtbt[:], dtb_in)
    ld_small = s_ld.n

    for i in range(ND):
        b.dma("sync", s_ld, xtmp[i][:], x_in[i * 128:(i + 1) * 128, :])

    W1 = []
    for wi_, (wsrc_, ntil_, wid_) in enumerate(((wg_sh, 8, 1024), (wu_sh, 8, 1024), (wd_sh, 32, 256))):
        for it_ in range(ntil_):
            W1.append((wsrc_, wi_, it_, wid_))
    NW1 = len(W1)
    wstP = [regABf[:, 16608 + 2056 * j:16608 + 2056 * j + 2048] for j in range(6)]
    asum4 = regABf[:, 28960:28964]
    absacc3 = regABf[:, 28968:28971]
    wdum = [regAB[:, 58240 + 2048 * j:58240 + 2048 * (j + 1)] for j in range(2)]
    s_wl1 = [Sem(nc, es, "wl1_%d" % j) for j in range(6)]
    s_p = Sem(nc, es, "pacc")
    def w1_src(wsrc, it, wid):
        return wsrc.rearrange("(n p) f -> n p f", p=128)[2 * it:2 * it + 2]
    def w1_dst(j, wid):
        return wstP[j][:, 0:2 * wid].rearrange("p (n f) -> p n f", n=2)
    pz29 = s_p.inc()
    b.emit("gpsimd", lambda e, h=s_p.h: e.memset(absacc3[:], 0.0).then_inc(h, 1))
    w1_abs = {}
    w1_acc = {}
    for gi in range(6):
        wsrc_, wi_, it_, wid_ = W1[gi]
        b.dma("sync", s_wl1[gi], w1_dst(gi, wid_), w1_src(wsrc_, it_, wid_))
    w1_gi = [0]
    def emit_w1_step():
        gi = w1_gi[0]
        if gi >= NW1:
            return
        w1_gi[0] += 1
        wsrc_, wi_, it_, wid_ = W1[gi]
        j = gi % 6
        b.wait("scalar", s_wl1[j], s_wl1[j].n)
        b.wait("scalar", s_p, w1_acc[gi - 4] if gi >= 4 else pz29)
        ss = s_s.inc()
        b.emit("scalar", lambda e, j=j, wid=wid_, q=gi % 4, h=s_s.h: e.activation(
            out=wdum[j % 2][:, 0:2 * wid], in_=wstP[j][:, 0:2 * wid], func=AF.Abs,
            accum_out=asum4[:, q:q + 1]).then_inc(h, 1))
        w1_abs[gi] = ss
        b.wait("gpsimd", s_s, ss)
        pp = s_p.inc()
        def w1acc(e, wi=wi_, q=gi % 4, h=s_p.h):
            e.drain()
            e.tensor_add(out=absacc3[:, wi:wi + 1], in0=absacc3[:, wi:wi + 1],
                         in1=asum4[:, q:q + 1]).then_inc(h, 1)
        b.emit("gpsimd", w1acc)
        w1_acc[gi] = pp
        if gi + 6 < NW1:
            wsrc2, wi2, it2, wid2 = W1[gi + 6]
            b.wait("sync", s_s, ss)
            b.dma("sync", s_wl1[j], w1_dst(j, wid2), w1_src(wsrc2, it2, wid2))

    tdone = {}
    b.wait("vector", s_ld, s_ld.n)
    for i in range(ND):
        j = i % 2
        if i >= 2:
            b.wait("vector", s_t, tdone[i - 2])
        v1 = s_v.inc()
        b.emit("vector", lambda e, i=i, j=j, h=s_v.h: e.tensor_copy(out=xbf[j][:], in_=xtmp[i][:]).then_inc(h, 1))
        b.wait("tensor", s_v, v1)
        t1 = s_t.inc()
        for ci, ((c0, cw), pt) in enumerate(zip(TCH, (pb[0], pb[1], pb[4]))):
            b.emit("tensor", lambda e, pt=pt, c0=c0, cw=cw, i=i, j=j, last=(ci == 2), h=s_t.h: (
                e.matmul(pt[:, 0:cw], lhsT=ones_sb[:], rhs=xbf[j][:, c0:c0 + cw], start=(i == 0), stop=(i == ND - 1)).then_inc(h, 1) if last else
                e.matmul(pt[:, 0:cw], lhsT=ones_sb[:], rhs=xbf[j][:, c0:c0 + cw], start=(i == 0), stop=(i == ND - 1))))
        b.wait("vector", s_t, t1)
        v2 = s_v.inc()
        b.emit("vector", lambda e, i=i, j=j, h=s_v.h: e.tensor_mul(out=xbf[j][:], in0=xtmp[i][:], in1=xtmp[i][:]).then_inc(h, 1))
        b.wait("tensor", s_v, v2)
        t2 = s_t.inc()
        for ci, ((c0, cw), pt) in enumerate(zip(TCH, (pb[2], pb[3], pb[5]))):
            b.emit("tensor", lambda e, pt=pt, c0=c0, cw=cw, i=i, j=j, last=(ci == 2), h=s_t.h: (
                e.matmul(pt[:, 0:cw], lhsT=ones_sb[:], rhs=xbf[j][:, c0:c0 + cw], start=(i == 0), stop=(i == ND - 1)).then_inc(h, 1) if last else
                e.matmul(pt[:, 0:cw], lhsT=ones_sb[:], rhs=xbf[j][:, c0:c0 + cw], start=(i == 0), stop=(i == ND - 1))))
        tdone[i] = t2

    inv_d = float(np.float32(1.0 / D))
    b.wait("vector", s_t, s_t.n)
    def fin_stats(e, h=s_v.h):
        e.tensor_scalar(out=mbc[:, 0:512], in0=pb[0][:], scalar1=inv_d, scalar2=None, op0=ALU.mult)
        e.tensor_scalar(out=mbc[:, 512:1024], in0=pb[1][:], scalar1=inv_d, scalar2=None, op0=ALU.mult)
        e.tensor_scalar(out=mbc[:, 1024:1027], in0=pb[4][:, 0:3], scalar1=inv_d, scalar2=None, op0=ALU.mult)
        e.tensor_scalar(out=rbc[:, 0:512], in0=pb[2][:], scalar1=inv_d, scalar2=None, op0=ALU.mult)
        e.tensor_scalar(out=rbc[:, 512:1024], in0=pb[3][:], scalar1=inv_d, scalar2=None, op0=ALU.mult)
        e.tensor_scalar(out=rbc[:, 1024:1027], in0=pb[5][:, 0:3], scalar1=inv_d, scalar2=None, op0=ALU.mult)
        e.drain()
        e.tensor_mul(out=cv0[:], in0=mbc[:], in1=mbc[:])
        e.drain()
        e.tensor_sub(out=rbc[:], in0=rbc[:], in1=cv0[:]).then_inc(h, 1)
    v_stats = s_v.inc()
    b.emit("vector", fin_stats)
    b.wait("scalar", s_v, v_stats)
    def fin_rstd(e, h=s_s.h):
        e.activation(out=rbc[:], in_=rbc[:], func=AF.Ln, bias=float(EPS), scale=1.0)
        e.drain()
        e.activation(out=rbc[:], in_=rbc[:], func=AF.Exp, bias=0.0, scale=-0.5).then_inc(h, 1)
    s_rstd = s_s.inc()
    b.emit("scalar", fin_rstd)
    b.wait("vector", s_s, s_rstd)
    for i in range(ND):
        def lnx_tile(e, i=i, h=s_v.h):
            e.drain()
            e.tensor_sub(out=cv0[:], in0=xtmp[i][:], in1=mbc[:])
            e.drain()
            e.tensor_mul(out=lnx1[i][:], in0=cv0[:], in1=rbc[:]).then_inc(h, 1)
        s_v.inc()
        b.emit("vector", lnx_tile)
    v_lnx1 = s_v.n

    if "lnx1" in debug:
        dbgl = b.dbg("lnx1", [D, TH])
        b.wait("gpsimd", s_v, v_lnx1)
        for i in range(ND):
            b.dma("gpsimd", s_gst, dbgl[i * 128:(i + 1) * 128, :], lnx1[i][:])

    b.barrier()

    def panel_src(w, f0, fw=128):
        return w[:, f0:f0 + fw].rearrange("(k p) f -> p k f", p=128)
    def panel_dst(j, fw=128):
        return wpan[j].ap().rearrange("p (k f) -> p k f", f=128)[:, :, 0:fw]

    s_wpb = [Sem(nc, es, "wp0"), Sem(nc, es, "wp1")]
    b.dma("gpsimd", s_wpb[0], panel_dst(0), panel_src(w_xbc, 0))
    NT_ALL = NF_XBC + 1
    evd = {}
    tmm = {}
    xpr = xpart.rearrange("p (c i t) -> p c i t", c=NCH, i=NF_G)
    for fi in range(NT_ALL):
        j = fi % 2
        r = fi % 2
        if fi + 1 < NT_ALL:
            jn = (fi + 1) % 2
            if fi >= 1:
                b.wait("gpsimd", s_t, tmm[fi - 1])
            if fi + 1 < NF_XBC:
                b.dma("gpsimd", s_wpb[jn], panel_dst(jn), panel_src(w_xbc, (fi + 1) * 128))
            else:
                b.dma("gpsimd", s_wpb[jn], panel_dst(jn, 64), panel_src(w_dt, 0, 64))
        b.wait("tensor", s_wpb[j], s_wpb[j].n)
        if fi == 0:
            b.wait("tensor", s_v, v_lnx1)
        if fi >= 2:
            b.wait("tensor", s_s, evd[fi - 2])
        fw = 128 if fi < NF_XBC else 64
        pA, pB, pT = pb[3 * r], pb[3 * r + 1], pb[3 * r + 2]
        ti = s_t.inc()
        for k in range(ND):
            lt = wpan[j][:, k * 128:k * 128 + fw]
            st = (k == 0); sp = (k == ND - 1)
            b.emit("tensor", lambda e, lt=lt, k=k, st=st, sp=sp, pA=pA, fw=fw: e.matmul(
                pA[0:fw, 0:512], lhsT=lt, rhs=lnx1[k][:, 0:512], start=st, stop=sp))
            b.emit("tensor", lambda e, lt=lt, k=k, st=st, sp=sp, pB=pB, fw=fw: e.matmul(
                pB[0:fw, 0:512], lhsT=lt, rhs=lnx1[k][:, 512:1024], start=st, stop=sp))
            b.emit("tensor", lambda e, lt=lt, k=k, st=st, sp=sp, pT=pT, fw=fw, last=(k == ND - 1), h=s_t.h: (
                e.matmul(pT[0:fw, 0:3], lhsT=lt, rhs=lnx1[k][:, 1024:1027], start=st, stop=sp).then_inc(h, 1) if last else
                e.matmul(pT[0:fw, 0:3], lhsT=lt, rhs=lnx1[k][:, 1024:1027], start=st, stop=sp)))
        tmm[fi] = ti
        b.wait("scalar", s_t, ti)
        if fi >= 1:
            b.wait("scalar", s_v, cvd_prev)
        if fi < NF_XBC:
            def evac(e, pA=pA, pB=pB, pT=pT, h=s_s.h):
                e.copy(out=cv0[:, 0:512], in_=pA[:])
                e.copy(out=cv0[:, 512:1024], in_=pB[:])
                e.copy(out=cv0[:, 1024:1027], in_=pT[:, 0:3]).then_inc(h, 1)
            ev = s_s.inc()
            b.emit("scalar", evac)
            evd[fi] = ev
            b.wait("vector", s_s, ev)
            def conv(e, fi=fi, h=s_v.h):
                e.tensor_scalar(out=cv1[:], in0=cv0[:, 0:T], scalar1=cwt[:, 4 * fi:4 * fi + 1], scalar2=None, op0=ALU.mult)
                for k in range(1, 4):
                    e.drain()
                    e.scalar_tensor_tensor(out=cv1[:], in0=cv0[:, k:k + T], scalar=cwt[:, 4 * fi + k:4 * fi + k + 1],
                                           in1=cv1[:], op0=ALU.mult, op1=ALU.add)
                e.drain().then_inc(h, 1)
            cvd = s_v.inc()
            cvd_prev = cvd
            b.emit("vector", conv)
            b.wait("scalar", s_v, cvd)
            if fi < NF_G:
                dst = xpr[:, :, fi, :]
            elif fi == NF_G:
                dst = Bc[:]
            else:
                dst = Cc[:]
            def silu(e, fi=fi, dst=dst, h=s_s.h):
                e.activation(out=dst, in_=cv1[:], func=AF.Silu, bias=cbt[:, fi:fi + 1], scale=1.0).then_inc(h, 1)
            s_s.inc()
            b.emit("scalar", silu)
        else:
            def evac_dt(e, pA=pA, pB=pB, pT=pT, h=s_s.h):
                e.activation(out=dt32[:, 0:512], in_=pA[0:64, :], func=AF.Identity, bias=dtbt[:], scale=1.0)
                e.activation(out=dt32[:, 512:1024], in_=pB[0:64, :], func=AF.Identity, bias=dtbt[:], scale=1.0)
                e.activation(out=dt32[:, 1024:1027], in_=pT[0:64, 0:3], func=AF.Identity, bias=dtbt[:], scale=1.0).then_inc(h, 1)
            ev = s_s.inc()
            b.emit("scalar", evac_dt)
            evd[fi] = ev
            b.wait("vector", s_s, ev)
            vv = s_v.inc()
            def dtmin(e, h=s_v.h):
                e.tensor_scalar(out=cv0[0:64, 0:TH], in0=dt32[:], scalar1=10.0, scalar2=None, op0=ALU.min).then_inc(h, 1)
            b.emit("vector", dtmin)
            b.wait("scalar", s_v, vv)
            def soft(e, h=s_s.h):
                e.activation(out=cv0[0:64, 0:TH], in_=cv0[0:64, 0:TH], func=AF.Exp)
                e.drain()
                e.activation(out=cv0[0:64, 0:TH], in_=cv0[0:64, 0:TH], func=AF.Ln, bias=1.0).then_inc(h, 1)
            sv = s_s.inc()
            b.emit("scalar", soft)
            b.wait("vector", s_s, sv)
            s_v.inc()
            def dtmax(e, h=s_v.h):
                e.tensor_max(out=dt32[:], in0=dt32[:], in1=cv0[0:64, 0:TH]).then_inc(h, 1)
            b.emit("vector", dtmax)
        emit_w1_step()
        if fi % 2 == 0:
            emit_w1_step()
    while w1_gi[0] < NW1:
        emit_w1_step()

    b.barrier()

    WSH = [(wg_sh, wqg_in, 16, 1024, 0), (wu_sh, wqu_in, 16, 1024, 1), (wd_sh, wqd_in, 64, 256, 2)]
    s_wl = [Sem(nc, es, "wl0"), Sem(nc, es, "wl1")]
    def shard_src(wsrc, it, wid):
        if wid == 1024:
            return wsrc[it * 128:(it + 1) * 128, :]
        return wsrc.rearrange("(n p) o -> n p o", p=128)[it, :, :]
    vz29 = s_v.inc()
    def z29(e, h=s_v.h):
        e.memset(mrow[:], 0.0).then_inc(h, 1)
    b.emit("vector", z29)
    b.wait("tensor", s_v, vz29)
    b.wait("tensor", s_p, s_p.n)
    b.wait("scalar", s_v, vz29)
    tps = s_t.inc()
    def psum3(e, h=s_t.h):
        e.matmul(pb[7][0:1, 0:1], lhsT=absacc3[:, 0:1], rhs=onescol[:], start=True, stop=True)
        e.matmul(pb[7][0:1, 1:2], lhsT=absacc3[:, 1:2], rhs=onescol[:], start=True, stop=True)
        e.matmul(pb[7][0:1, 2:3], lhsT=absacc3[:, 2:3], rhs=onescol[:], start=True, stop=True).then_inc(h, 1)
    b.emit("tensor", psum3)
    b.wait("scalar", s_t, tps)
    sps = s_s.inc()
    b.emit("scalar", lambda e, h=s_s.h: e.copy(out=mrow[:, 0:3], in_=pb[7][0:1, 0:3]).then_inc(h, 1))
    b.wait("sync", s_s, sps)
    b.dma("sync", s_wl[0], ar_in.ap(), mrow[:, 0:4])
    b.wait("gpsimd", s_wl[0], s_wl[0].n)
    s_cw = Sem(nc, es, "cw")
    s_cw.inc()
    b.emit("gpsimd", lambda e, h=s_cw.h: e.collective_compute(
        "AllReduce", ALU.add, replica_groups=[[0, 1, 2, 3, 4, 5, 6, 7]],
        ins=[ar_in.ap()], outs=[ar_out.ap()]).then_inc(h, 1))
    b.wait("sync", s_cw, s_cw.n)
    b.dma("sync", s_wl[1], mrow[:, 4:8], ar_out.ap())
    b.wait("vector", s_wl[1], s_wl[1].n)
    vmw = s_v.inc()
    NELG = float(FFI * D)
    def mkthr(e, h=s_v.h):
        e.tensor_scalar(out=mrow[:, 12:16], in0=mrow[:, 4:8], scalar1=1.0 / NELG, scalar2=float(EPS), op0=ALU.mult, op1=ALU.max)
        e.drain()
        e.reciprocal(out=mrow[:, 8:12], in_=mrow[:, 12:16]).then_inc(h, 1)
    b.emit("vector", mkthr)
    b.wait("tensor", s_v, vmw)
    tbc = s_t.inc()
    b.emit("tensor", lambda e, h=s_t.h: e.matmul(pb[7][:, 0:8], lhsT=ones1f[0:1, :], rhs=mrow[0:1, 8:16], start=True, stop=True).then_inc(h, 1))
    b.wait("scalar", s_t, tbc)
    swq = s_s.inc()
    b.emit("scalar", lambda e, h=s_s.h: e.copy(out=wqrow[:], in_=pb[7][:, 0:8]).then_inc(h, 1))
    b.wait("vector", s_s, swq)
    s_cg = Sem(nc, es, "cg")
    b.barrier()

    SCB = 32768
    S_sb = regAB[:, SCB:SCB + HEADS * P]
    EGk = regAB[:, SCB + 4096:SCB + 4096 + HEADS * NCH]
    scoresM = regAB[:, SCB + 4608:SCB + 4608 + Q]
    Btm = regAB[:, SCB + 4736:SCB + 4736 + Q]
    dtx4 = [regAB[:, SCB + 4864 + 256 * j: SCB + 5120 + 256 * j] for j in range(2)]
    dtxw4 = [regAB[:, SCB + 5376 + 256 * j: SCB + 5632 + 256 * j] for j in range(2)]
    SCBf = 16384
    ytmp4 = regABf[:, SCBf + 2944:SCBf + 2944 + 256]
    zer64 = regABf[0:64, SCBf + 3200:SCBf + 3200 + Q]
    afm = regABf[0:64, SCBf + 3328:SCBf + 3328 + Q]
    cumfm = regABf[0:64, SCBf + 3456:SCBf + 3456 + Q]
    cumflat = regABf[0:33, SCBf + 3584:SCBf + 3584 + 2048]
    DTm4 = [regABf[:, SCBf + 5632 + 512 * j: SCBf + 6144 + 512 * j] for j in range(2)]
    M4 = [regAB[:, SCB + 13312 + 512 * j: SCB + 13824 + 512 * j] for j in range(2)]
    gstg = [b.sb(f"gstg{j}", [128, T], BF16, D0 + 9824 + 2048 * j) for j in range(2)]
    x_tm = regE[:, 0:INTER]
    dttm = regEf[:, 2048:2112]
    cumtm = regEf[:, 2112:2176]
    ncumtm = regEf[:, 2176:2240]
    Etm = regEf[:, 2240:2304]
    wtm = regEf[:, 2304:2368]
    etbc = regEf[:, 2368:2432]
    g_run = regEf[:, 2432:2496]
    W4 = [regE[:, 4992 + 512 * j: 5504 + 512 * j] for j in range(2)]
    ytm = [regAB[:, c * 4096:(c + 1) * 4096] for c in range(NCH)]

    b.wait("vector", s_ld, s_ld.n)
    vz = s_v.inc()
    def scan_init(e, h=s_v.h):
        e.memset(zer64[:], 0.0)
        e.memset(S_sb[:], 0.0)
        e.memset(g_run[:], 1.0).then_inc(h, 1)
    b.emit("vector", scan_init)
    sA = s_s.inc()
    def mkA(e, h=s_s.h):
        e.activation(out=abtA[:], in_=abt[:], func=AF.Exp).then_inc(h, 1)
    b.emit("scalar", mkA)
    b.wait("vector", s_s, sA)
    vA = s_v.inc()
    def negA(e, h=s_v.h):
        e.tensor_scalar(out=abtA[:], in0=abtA[:], scalar1=-1.0, scalar2=None, op0=ALU.mult).then_inc(h, 1)
    b.emit("vector", negA)
    b.wait("tensor", s_ld, s_ld.n)
    b.wait("tensor", s_v, vz)
    tdb = s_t.inc()
    b.emit("tensor", lambda e, h=s_t.h: e.matmul(pb[7][:, 0:64], lhsT=ones1f[0:1, :], rhs=dvr[:], start=True, stop=True).then_inc(h, 1))
    b.wait("scalar", s_t, tdb)
    sdb = s_s.inc()
    b.emit("scalar", lambda e, h=s_s.h: e.copy(out=Dbc[:], in_=pb[7][:, 0:64]).then_inc(h, 1))
    b.wait("tensor", s_s, sdb)
    b.barrier()


    def amax_bcast(amr_ap, out_ap):
        b.wait("tensor", s_v, s_v.n)
        for c in range(8):
            pk = pb[6 + c % 2]
            if c >= 2:
                b.wait("tensor", s_v, ab_v[c - 2])
            tv = s_t.inc()
            b.emit("tensor", lambda e, c=c, pk=pk, amr_ap=amr_ap, h=s_t.h: e.transpose(
                pk[:, 0:128], in_=amr_ap[:, 128 * c:128 * c + 128], identity=identf[:]).then_inc(h, 1))
            b.wait("vector", s_t, tv)
            vv = s_v.inc()
            def red(e, c=c, pk=pk, h=s_v.h):
                e.drain()
                e.reduce_max(out=amtm[:, c:c + 1], in_=pk[:, 0:128], axis=mybir.AxisListType.X).then_inc(h, 1)
            b.emit("vector", red)
            if c == 0:
                ab_v = {}
            ab_v[c] = vv
        b.wait("vector", s_v, s_v.n)
        ab_t = {}
        for c in range(8):
            pk = pb[6 + c % 2]
            if c >= 2:
                b.wait("vector", s_s, ab_s[c - 2])
            if c >= 1:
                b.wait("vector", s_t, ab_t[c - 1])
            vv = s_v.inc()
            def colb(e, c=c, h=s_v.h):
                e.drain()
                e.tensor_scalar(out=coltl[:], in0=ones_sb[:], scalar1=amtm[:, c:c + 1], scalar2=None, op0=ALU.mult).then_inc(h, 1)
            b.emit("vector", colb)
            b.wait("tensor", s_v, vv)
            tv = s_t.inc()
            ab_t[c] = tv
            b.emit("tensor", lambda e, pk=pk, h=s_t.h: e.transpose(pk[:, 0:128], in_=coltl[:], identity=identf[:]).then_inc(h, 1))
            b.wait("scalar", s_t, tv)
            sv = s_s.inc()
            b.emit("scalar", lambda e, c=c, pk=pk, out_ap=out_ap, h=s_s.h: e.copy(
                out=out_ap[:, 128 * c:128 * c + 128], in_=pk[:, 0:128]).then_inc(h, 1))
            if c == 0:
                ab_s = {}
            ab_s[c] = sv
        b.wait("vector", s_s, s_s.n)

    s_cf = Sem(nc, es, "cf")
    pb2bf = [pb[i][:].bitcast(BF16) for i in range(8)]

    pm = {"vTS": [0] * 16, "vEv": [0] * 16, "sExp": [0] * 16, "vW": [0] * 16, "tY": [0] * 16,
          "tP4": 0, "tP8": 0, "cf": 0}

    def scan_chunk(c):
        dts = dt32[:, 3 + c * Q: 3 + (c + 1) * Q]
        Bsl = Bc[:, c * Q:(c + 1) * Q]
        Csl = Cc[:, c * Q:(c + 1) * Q]
        vTS = [0] * 16; vEv = [0] * 16
        if pm["vEv"][15]:
            b.wait("scalar", s_v, pm["vEv"][15])
        if pm["tP8"]:
            b.wait("scalar", s_t, pm["tP8"])
        if pm["tP4"]:
            b.wait("vector", s_t, pm["tP4"])
        if pm["cf"]:
            b.wait("vector", s_cf, pm["cf"])
        vP1 = s_v.inc()
        def p1(e, dts=dts, h=s_v.h):
            e.tensor_scalar(out=afm[:], in0=dts, scalar1=abtA[:], scalar2=None, op0=ALU.mult)
            e.drain()
            e.tensor_tensor_scan(out=cumfm[:], data0=afm[:], data1=zer64[:], initial=0.0, op0=ALU.add, op1=ALU.add).then_inc(h, 1)
        b.emit("vector", p1)
        b.wait("sync", s_v, vP1)
        b.dma("sync", s_cf, cumflat[0:1, :].rearrange("p (h t) -> p h t", t=Q), cumfm[0:16, :])
        b.dma("sync", s_cf, cumflat[32:33, :].rearrange("p (h t) -> p h t", t=Q), cumfm[16:32, :])
        cfA = s_cf.n
        trs_ev = {}
        for rnd in range(4):
            bk = rnd % 2
            if rnd < 2:
                b.wait("tensor", s_v, pm["vEv"][14 + bk] or 0) if pm["vEv"][14 + bk] else None
            else:
                b.wait("tensor", s_s, trs_ev[rnd - 2])
            tr = s_t.inc()
            for ii in range(8):
                i = rnd * 8 + ii
                b.emit("tensor", lambda e, c=c, i=i, ii=ii, bk=bk, last=(ii == 7), h=s_t.h: (
                    e.transpose(pb2bf[bk][:, 128 * ii:128 * ii + 128], in_=xpr[:, c, i, :], identity=identb[:]).then_inc(h, 1)
                    if last else
                    e.transpose(pb2bf[bk][:, 128 * ii:128 * ii + 128], in_=xpr[:, c, i, :], identity=identb[:])))
            b.wait("scalar", s_t, tr)
            sv = s_s.inc()
            b.emit("scalar", lambda e, rnd=rnd, bk=bk, h=s_s.h: e.copy(
                out=x_tm[:, 1024 * rnd:1024 * rnd + 1024], in_=pb2bf[bk][:, 0:1024]).then_inc(h, 1))
            trs_ev[rnd] = sv
        if pm["vTS"][14]:
            b.wait("tensor", s_v, pm["vTS"][14])
        b.wait("tensor", s_v, vP1)
        tP4 = s_t.inc()
        def p4(e, dts=dts, h=s_t.h):
            e.transpose(pb[2][:, 0:64], in_=dts, identity=identf[0:64, 0:64])
            e.transpose(pb[2][:, 64:128], in_=cumfm[:], identity=identf[0:64, 0:64]).then_inc(h, 1)
        b.emit("tensor", p4)
        b.wait("scalar", s_t, tP4)
        sP5 = s_s.inc()
        def p5(e, h=s_s.h):
            e.copy(out=dttm[:], in_=pb[2][:, 0:64])
            e.copy(out=cumtm[:], in_=pb[2][:, 64:128])
            e.drain()
            e.activation(out=Etm[:], in_=cumtm[:], func=AF.Exp).then_inc(h, 1)
        b.emit("scalar", p5)
        b.wait("vector", s_s, sP5)
        vP6 = s_v.inc()
        def p6(e, h=s_v.h):
            e.drain()
            e.tensor_scalar(out=ncumtm[:], in0=cumtm[:], scalar1=-1.0, scalar2=None, op0=ALU.mult).then_inc(h, 1)
        b.emit("vector", p6)
        if pm["vTS"][15]:
            b.wait("tensor", s_v, pm["vTS"][15])
        b.wait("tensor", s_s, sP5)
        tP8 = s_t.inc()
        def p8(e, Bsl=Bsl, Csl=Csl, h=s_t.h):
            e.matmul(pb[3][:, 0:128], lhsT=Bsl, rhs=Csl, start=True, stop=True)
            e.matmul(pb[3][:, 128:192], lhsT=sel127[:], rhs=cumtm[:], start=True, stop=True)
            e.transpose(pb2bf[3][:, 384:512], in_=Bsl, identity=identb[:]).then_inc(h, 1)
        b.emit("tensor", p8)
        b.wait("vector", s_t, tP8)
        vP9 = s_v.inc()
        def p9v(e, c=c, h=s_v.h):
            e.tensor_tensor(out=scoresM[:], in0=pb[3][:, 0:128], in1=triu_m[:], op=ALU.mult)
            e.tensor_sub(out=wtm[:], in0=pb[3][:, 128:192], in1=cumtm[:])
            e.tensor_mul(out=EGk[:, 64 * c:64 * c + 64], in0=Etm[:], in1=g_run[:]).then_inc(h, 1)
        b.emit("vector", p9v)
        b.wait("scalar", s_v, vP9)
        sP9 = s_s.inc()
        def p9s(e, h=s_s.h):
            e.activation(out=wtm[:], in_=wtm[:], func=AF.Exp)
            e.activation(out=etbc[:], in_=pb[3][:, 128:192], func=AF.Exp)
            e.copy(out=Btm[:], in_=pb2bf[3][:, 384:512]).then_inc(h, 1)
        b.emit("scalar", p9s)
        b.wait("vector", s_s, sP9)
        vP9b = s_v.inc()
        def p9b(e, h=s_v.h):
            e.drain()
            e.tensor_mul(out=g_run[:], in0=g_run[:], in1=etbc[:]).then_inc(h, 1)
        b.emit("vector", p9b)

        cfB = cfA
        sExp = [0] * 16; vW = [0] * 16; tY = [0] * 16; t1s = [0] * 16
        for g in range(16):
            pi = g % 2
            DTb = pb[2 + pi]; yABb = pb[4 + pi]; dSb = pb[0 + pi]
            if g == 8:
                b.wait("sync", s_t, t1s[7])
                b.dma("sync", s_cf, cumflat[0:1, :].rearrange("p (h t) -> p h t", t=Q), cumfm[32:48, :])
                b.dma("sync", s_cf, cumflat[32:33, :].rearrange("p (h t) -> p h t", t=Q), cumfm[48:64, :])
                cfB = s_cf.n
            b.wait("tensor", s_cf, cfA if g < 8 else cfB)
            if g >= 2:
                b.wait("tensor", s_v, vTS[g - 2])
            elif pm["vTS"][g + 14]:
                b.wait("tensor", s_v, pm["vTS"][g + 14])
            gq = g % 8
            base = 0 if gq < 4 else 32
            col = 512 * (gq % 4)
            t1 = s_t.inc()
            b.emit("tensor", lambda e, DTb=DTb, base=base, col=col, h=s_t.h: e.matmul(
                DTb[:], lhsT=ones1f[base:base + 1, :], rhs=cumflat[base:base + 1, col:col + 512], start=True, stop=True).then_inc(h, 1))
            t1s[g] = t1
            b.wait("vector", s_t, t1)
            if g >= 2:
                b.wait("vector", s_s, sExp[g - 2])
                b.wait("vector", s_t, tY[g - 2])
            else:
                if pm["sExp"][g + 14]:
                    b.wait("vector", s_s, pm["sExp"][g + 14])
                if pm["tY"][g + 14]:
                    b.wait("vector", s_t, pm["tY"][g + 14])
            b.wait("vector", s_s, sP9)
            v1 = s_v.inc()
            def st2(e, g=g, pi=pi, DTb=DTb, h=s_v.h):
                for h4 in range(4):
                    hh = 4 * g + h4
                    e.tensor_scalar(out=DTm4[pi][:, 128 * h4:128 * h4 + 128], in0=DTb[:, 128 * h4:128 * h4 + 128],
                                    scalar1=ncumtm[:, hh:hh + 1], scalar2=0.0, op0=ALU.add, op1=ALU.min)
                e.tensor_tensor(out=dtx4[pi].rearrange("p (g4 q) -> p g4 q", q=64),
                                in0=x_tm[:, 256 * g:256 * g + 256].rearrange("p (g4 q) -> p g4 q", q=64),
                                in1=dttm[:, 4 * g:4 * g + 4].unsqueeze(2).broadcast_to([128, 4, 64]), op=ALU.mult)
                e.drain()
                e.tensor_tensor(out=dtxw4[pi].rearrange("p (g4 q) -> p g4 q", q=64),
                                in0=dtx4[pi].rearrange("p (g4 q) -> p g4 q", q=64),
                                in1=wtm[:, 4 * g:4 * g + 4].unsqueeze(2).broadcast_to([128, 4, 64]), op=ALU.mult)
                e.drain().then_inc(h, 1)
            b.emit("vector", st2)
            vTS[g] = v1
            b.wait("scalar", s_v, v1)
            if g >= 2:
                b.wait("scalar", s_v, vW[g - 2])
            elif pm["vW"][g + 14]:
                b.wait("scalar", s_v, pm["vW"][g + 14])
            s1 = s_s.inc()
            b.emit("scalar", lambda e, pi=pi, h=s_s.h: e.activation(out=M4[pi][:], in_=DTm4[pi][:], func=AF.Exp).then_inc(h, 1))
            sExp[g] = s1
            b.wait("vector", s_s, s1)
            v2 = s_v.inc()
            def st4(e, pi=pi, h=s_v.h):
                e.tensor_tensor(out=W4[pi].rearrange("p (g4 q) -> p g4 q", q=128),
                                in0=M4[pi].rearrange("p (g4 q) -> p g4 q", q=128),
                                in1=scoresM[:].unsqueeze(1).broadcast_to([128, 4, 128]), op=ALU.mult)
                e.drain().then_inc(h, 1)
            b.emit("vector", st4)
            vW[g] = v2
            b.wait("tensor", s_v, v2)
            if g >= 2:
                b.wait("tensor", s_v, vEv[g - 2])
            elif pm["vEv"][g + 14]:
                b.wait("tensor", s_v, pm["vEv"][g + 14])
            t2 = s_t.inc()
            def st5(e, g=g, pi=pi, yABb=yABb, dSb=dSb, Csl=Csl, h=s_t.h):
                for h4 in range(4):
                    hh = 4 * g + h4
                    e.matmul(yABb[:, 64 * h4:64 * h4 + 64], lhsT=W4[pi][:, 128 * h4:128 * h4 + 128],
                             rhs=dtx4[pi][:, 64 * h4:64 * h4 + 64], start=True, stop=True)
                    e.matmul(yABb[:, 256 + 64 * h4:256 + 64 * h4 + 64], lhsT=Csl,
                             rhs=S_sb[:, 64 * hh:64 * hh + 64], start=True, stop=True)
                    ins = e.matmul(dSb[:, 64 * h4:64 * h4 + 64], lhsT=Btm[:],
                                   rhs=dtxw4[pi][:, 64 * h4:64 * h4 + 64], start=True, stop=True)
                    if h4 == 3:
                        ins.then_inc(h, 1)
            b.emit("tensor", st5)
            tY[g] = t2
            emit_gate_piece()
            b.wait("vector", s_t, t2)
            v3 = s_v.inc()
            def st6(e, c=c, g=g, pi=pi, yABb=yABb, dSb=dSb, h=s_v.h):
                e.drain()
                for h4 in range(4):
                    hh = 4 * g + h4
                    e.tensor_scalar(out=ytmp4[:, 64 * h4:64 * h4 + 64], in0=yABb[:, 256 + 64 * h4:256 + 64 * h4 + 64],
                                    scalar1=Etm[:, hh:hh + 1], scalar2=None, op0=ALU.mult)
                e.drain()
                e.tensor_add(out=ytmp4[:], in0=ytmp4[:], in1=yABb[:, 0:256])
                e.drain()
                for h4 in range(4):
                    hh = 4 * g + h4
                    e.scalar_tensor_tensor(out=ytm[c][:, 64 * hh:64 * hh + 64], in0=x_tm[:, 64 * hh:64 * hh + 64],
                                           scalar=Dbc[:, hh:hh + 1], in1=ytmp4[:, 64 * h4:64 * h4 + 64],
                                           op0=ALU.mult, op1=ALU.add)
                for h4 in range(4):
                    hh = 4 * g + h4
                    e.tensor_scalar(out=S_sb[:, 64 * hh:64 * hh + 64], in0=S_sb[:, 64 * hh:64 * hh + 64],
                                    scalar1=etbc[:, hh:hh + 1], scalar2=None, op0=ALU.mult)
                e.drain()
                e.tensor_add(out=S_sb[:, 256 * g:256 * g + 256], in0=S_sb[:, 256 * g:256 * g + 256], in1=dSb[:, 0:256]).then_inc(h, 1)
            b.emit("vector", st6)
            vEv[g] = v3
        pm["vTS"] = vTS; pm["vEv"] = vEv; pm["sExp"] = sExp; pm["vW"] = vW; pm["tY"] = tY
        pm["tP4"] = tP4; pm["tP8"] = tP8; pm["cf"] = s_cf.n

    s_gs = [Sem(nc, es, "gs0"), Sem(nc, es, "gs1")]
    gstate = {"fi": 0, "q": 0}
    gmk = {"t": {}, "s": {}}
    b.wait("gpsimd", s_t, tmm[NT_ALL - 1])
    b.dma("gpsimd", s_wpb[0], panel_dst(0), panel_src(w_z, 0))
    b.dma("gpsimd", s_wpb[1], panel_dst(1), panel_src(w_z, 128))
    gp_cnt = [s_wpb[0].n, s_wpb[1].n]

    def emit_gate_piece():
        fi, q = gstate["fi"], gstate["q"]
        if fi >= NF_G:
            return
        j = fi % 2
        if q == 0:
            b.wait("tensor", s_wpb[j], gp_cnt[j])
            if fi >= 1:
                b.wait("tensor", s_s, gmk["s"][fi - 1])
            if fi >= 2:
                b.wait("tensor", s_gs[j], s_gs[j].n)
        tmark = s_t.inc()
        for k in range(4 * q, 4 * q + 4):
            st = (k == 0); sp = (k == ND - 1)
            b.emit("tensor", lambda e, j=j, k=k, st=st, sp=sp: e.matmul(
                pb[6][:], lhsT=wpan[j][:, k * 128:(k + 1) * 128], rhs=lnx1[k][:, 3:515], start=st, stop=sp))
            b.emit("tensor", lambda e, j=j, k=k, st=st, sp=sp, last=(k == 4 * q + 3), h=s_t.h: (
                e.matmul(pb[7][:], lhsT=wpan[j][:, k * 128:(k + 1) * 128], rhs=lnx1[k][:, 515:1027], start=st, stop=sp).then_inc(h, 1) if last else
                e.matmul(pb[7][:], lhsT=wpan[j][:, k * 128:(k + 1) * 128], rhs=lnx1[k][:, 515:1027], start=st, stop=sp)))
        if q == 3:
            gmk["t"][fi] = tmark
            b.wait("scalar", s_t, tmark)
            sm = s_s.inc()
            def gevac(e, fi=fi, j=j, h=s_s.h):
                e.copy(out=gstg[j][:, 0:512], in_=pb[6][:])
                e.copy(out=gstg[j][:, 512:1024], in_=pb[7][:]).then_inc(h, 1)
            b.emit("scalar", gevac)
            gmk["s"][fi] = sm
            b.wait("sync", s_s, sm)
            b.dma("sync", s_gs[j], gate_dram[fi * 128:(fi + 1) * 128, :], gstg[j][:])
            if fi + 2 < NF_G:
                b.wait("gpsimd", s_t, tmark)
                jn = (fi + 2) % 2
                b.dma("gpsimd", s_wpb[jn], panel_dst(jn), panel_src(w_z, (fi + 2) * 128))
                gp_cnt[jn] = s_wpb[jn].n
            gstate["fi"] += 1
            gstate["q"] = 0
        else:
            gstate["q"] += 1

    for c in range(NCH):
        scan_chunk(c)
    while gstate["fi"] < NF_G:
        emit_gate_piece()
    b.wait("sync", s_gs[0], s_gs[0].n)
    b.wait("sync", s_gs[1], s_gs[1].n)

    b.barrier()

    agS_in = nc.dram_tensor("agS_in", [128, HEADS * P], BF16, kind="Internal")
    agS_out = nc.dram_tensor("agS_out", [256, HEADS * P], BF16, kind="Internal")
    S_in = regE[:, 0:HEADS * P]
    b.dma("sync", s_ld, smt[:], smask)
    b.dma("sync", s_st, agS_in.ap(), S_sb[:])
    b.wait("gpsimd", s_st, s_st.n)
    s_cc = Sem(nc, es, "cc")
    s_cc.inc()
    b.emit("gpsimd", lambda e, h=s_cc.h: e.collective_compute(
        "AllGather", ALU.bypass, replica_groups=[[0, 1], [2, 3], [4, 5], [6, 7]],
        ins=[agS_in.ap()], outs=[agS_out.ap()]).then_inc(h, 1))
    b.wait("sync", s_cc, s_cc.n)
    b.dma("sync", s_ld, S_in[:], agS_out.ap()[0:128, :])
    b.wait("vector", s_ld, s_ld.n)
    vsi = s_v.inc()
    def mask_sin(e, h=s_v.h):
        e.tensor_scalar(out=S_in[:], in0=S_in[:], scalar1=smt[:], scalar2=None, op0=ALU.mult).then_inc(h, 1)
    b.emit("vector", mask_sin)
    b.wait("tensor", s_v, vsi)
    corr_v = {}
    for c in range(NCH):
        for g in range(16):
            pk = pb[4 + (g % 2)]
            if c * 16 + g >= 2:
                b.wait("tensor", s_v, corr_v[c * 16 + g - 2])
            ti = s_t.inc()
            def corr_mm(e, c=c, g=g, pk=pk, h=s_t.h):
                for h4 in range(4):
                    hh = 4 * g + h4
                    ins = e.matmul(pk[:, 64 * h4:64 * h4 + 64], lhsT=Cc[:, c * Q:(c + 1) * Q],
                                   rhs=S_in[:, 64 * hh:64 * hh + 64], start=True, stop=True)
                    if h4 == 3:
                        ins.then_inc(h, 1)
            b.emit("tensor", corr_mm)
            b.wait("vector", s_t, ti)
            vv2 = s_v.inc()
            def corr_ev(e, c=c, g=g, pk=pk, h=s_v.h):
                e.drain()
                for h4 in range(4):
                    hh = 4 * g + h4
                    e.scalar_tensor_tensor(out=ytm[c][:, 64 * hh:64 * hh + 64], in0=pk[:, 64 * h4:64 * h4 + 64],
                                           scalar=EGk[:, 64 * c + hh:64 * c + hh + 1],
                                           in1=ytm[c][:, 64 * hh:64 * hh + 64], op0=ALU.mult, op1=ALU.add)
                e.drain().then_inc(h, 1)
            corr_v[c * 16 + g] = vv2
            b.emit("vector", corr_ev)
    b.barrier()

    if "scan" in debug:
        dbgy2 = b.dbg("ytmdbg", [NCH * 128, INTER])
        for c in range(NCH):
            b.dma("gpsimd", s_gst, dbgy2[c * 128:(c + 1) * 128, :], ytm[c][:])
        b.wait("gpsimd", s_gst, s_gst.n)

    gate_t = [regAB[:, SCB + i2 * 1024:SCB + (i2 + 1) * 1024] for i2 in range(NF_G)]
    s_gl = [Sem(nc, es, "gl0"), Sem(nc, es, "gl1")]
    b.dma("sync", s_gl[0], gstg[0][:], gate_dram[0:128, :])
    b.dma("sync", s_gl[1], gstg[1][:], gate_dram[128:256, :])
    silg = regE[:, 0:1024]
    sqt = regE[:, 1024:2048]
    rbcg = regEf[:, 1024:2048]
    resid = [regABf[:, i2 * 1024:(i2 + 1) * 1024] for i2 in range(ND)]
    b.dma("sync", s_ld, nwt[:], nw_in)

    W2 = []
    for wi_, (wsrc_, wdst_, ntil_, wid_, slot_) in enumerate(WSH):
        for it_ in range(ntil_):
            W2.append((wsrc_, wdst_, it_, wid_, slot_, wi_))
    NW2 = len(W2)
    wst2 = [regCf[:, 3104 + 1040 * k:3104 + 1040 * k + 1024] for k in range(3)]
    wqt2 = [regC[:, 12448 + 1024 * k:12448 + 1024 * (k + 1)] for k in range(3)]
    s_wl2 = [Sem(nc, es, "wl2_%d" % k) for k in range(3)]
    s_wq2 = [Sem(nc, es, "wq2_%d" % k) for k in range(3)]
    for gi2 in range(3):
        wsrc_, wdst_, it_, wid_, slot_, wi_ = W2[gi2]
        b.dma("sync", s_wl2[gi2], wst2[gi2][:, 0:wid_], shard_src(wsrc_, it_, wid_))
    w2_gi = [0]
    def emit_w2_step():
        gi = w2_gi[0]
        if gi >= NW2:
            return
        w2_gi[0] += 1
        wsrc_, wdst_, it_, wid_, slot_, wi_ = W2[gi]
        j = gi % 3
        b.wait("scalar", s_wl2[j], s_wl2[j].n)
        ss = s_s.inc()
        def w2s(e, j=j, wid=wid_, slot=slot_, h=s_s.h):
            e.activation(out=wst2[j][:, 0:wid], in_=wst2[j][:, 0:wid], func=AF.Identity,
                         bias=RC, scale=wqrow[:, slot:slot + 1])
            e.drain()
            e.activation(out=wst2[j][:, 0:wid], in_=wst2[j][:, 0:wid], func=AF.Identity,
                         bias=-RC).then_inc(h, 1)
        b.emit("scalar", w2s)
        b.wait("vector", s_s, ss)
        if gi >= 3:
            b.wait("vector", s_wq2[j], s_wq2[j].n)
        vv = s_v.inc()
        def w2v(e, j=j, wid=wid_, h=s_v.h):
            e.tensor_scalar(out=wqt2[j][:, 0:wid], in0=wst2[j][:, 0:wid], scalar1=1.0, scalar2=-1.0, op0=ALU.min, op1=ALU.max).then_inc(h, 1)
        b.emit("vector", w2v)
        b.wait("sync", s_v, vv)
        b.dma("sync", s_wq2[j], shard_src(wdst_.ap(), it_, wid_), wqt2[j][:, 0:wid_])
        if gi + 3 < NW2:
            wsrcN, wdstN, itN, widN, slotN, wiN = W2[gi + 3]
            b.dma("sync", s_wl2[j], wst2[j][:, 0:widN], shard_src(wsrcN, itN, widN))
        if wid_ == 1024 and it_ % 4 == 3:
            p = it_ // 4
            wout = wqg_out if wi_ == 0 else wqu_out
            for k3 in range(3):
                b.wait("gpsimd", s_wq2[k3], s_wq2[k3].n)
            s_cg.inc()
            b.emit("gpsimd", lambda e, win=wdst_, p=p, wout=wout, h=s_cg.h: e.collective_compute(
                "AllGather", ALU.bypass, replica_groups=[[0, 1, 2, 3, 4, 5, 6, 7]],
                ins=[win.ap()[p * 512:(p + 1) * 512, :]],
                outs=[wout.ap()[p * 4096:(p + 1) * 4096, :]]).then_inc(h, 1))
        if wi_ == 2 and it_ == 63:
            for k3 in range(3):
                b.wait("gpsimd", s_wq2[k3], s_wq2[k3].n)
            s_cg.inc()
            b.emit("gpsimd", lambda e, h=s_cg.h: e.collective_compute(
                "AllGather", ALU.bypass, replica_groups=[[0, 1, 2, 3, 4, 5, 6, 7]],
                ins=[wqd_in.ap()], outs=[wqd_out.ap()]).then_inc(h, 1))
    vhs = {}
    tsq = {}
    str_prev = {}
    for fi in range(NF_G):
        r4 = fi % 2
        if fi >= 2:
            b.wait("tensor", s_v, vhs[fi - 2])
        ttr = s_t.inc()
        for c in range(NCH):
            b.emit("tensor", lambda e, c=c, fi=fi, r4=r4, last=(c == NCH - 1), h=s_t.h: (
                e.transpose(pb2bf[r4][:, 128 * c:128 * c + 128], in_=ytm[c][:, fi * 128:(fi + 1) * 128],
                            identity=identb[:]).then_inc(h, 1) if last else
                e.transpose(pb2bf[r4][:, 128 * c:128 * c + 128], in_=ytm[c][:, fi * 128:(fi + 1) * 128],
                            identity=identb[:])))
        if fi >= 1:
            b.wait("scalar", s_v, vhs[fi - 1])
        b.wait("scalar", s_gl[fi % 2], s_gl[fi % 2].n)
        ssil = s_s.inc()
        b.emit("scalar", lambda e, fi=fi, h=s_s.h: e.activation(out=silg[:], in_=gstg[fi % 2][:], func=AF.Silu).then_inc(h, 1))
        if fi + 2 < NF_G:
            b.wait("sync", s_s, ssil)
            b.dma("sync", s_gl[fi % 2], gstg[fi % 2][:], gate_dram[(fi + 2) * 128:(fi + 3) * 128, :])
        b.wait("vector", s_t, ttr)
        b.wait("vector", s_s, ssil)
        if fi >= 1:
            b.wait("vector", s_t, tsq[fi - 1])
        vv = s_v.inc()
        def hsq(e, fi=fi, r4=r4, h=s_v.h):
            e.drain()
            e.tensor_mul(out=gate_t[fi][:], in0=pb2bf[r4][:, 0:1024], in1=silg[:])
            e.drain()
            e.tensor_mul(out=sqt[:], in0=gate_t[fi][:], in1=gate_t[fi][:]).then_inc(h, 1)
        b.emit("vector", hsq)
        vhs[fi] = vv
        b.wait("tensor", s_v, vv)
        ts2 = s_t.inc()
        for ci, (c0, cw) in enumerate(TCO):
            b.emit("tensor", lambda e, c0=c0, cw=cw, fi=fi, ci=ci, last=(ci == 1), h=s_t.h: (
                e.matmul(pb[2 + ci][:], lhsT=ones_sb[:], rhs=sqt[:, c0:c0 + cw], start=(fi == 0), stop=(fi == NF_G - 1)).then_inc(h, 1) if last else
                e.matmul(pb[2 + ci][:], lhsT=ones_sb[:], rhs=sqt[:, c0:c0 + cw], start=(fi == 0), stop=(fi == NF_G - 1))))
        tsq[fi] = ts2
        emit_w2_step()
    b.wait("vector", s_t, s_t.n)
    inv_i = float(np.float32(1.0 / INTER))
    vst4 = s_v.inc()
    def gstat(e, h=s_v.h):
        e.tensor_scalar(out=rbcg[:, 0:512], in0=pb[2][:], scalar1=inv_i, scalar2=None, op0=ALU.mult)
        e.tensor_scalar(out=rbcg[:, 512:1024], in0=pb[3][:], scalar1=inv_i, scalar2=None, op0=ALU.mult).then_inc(h, 1)
    b.emit("vector", gstat)
    b.wait("scalar", s_v, vst4)
    sst4 = s_s.inc()
    def grstd(e, h=s_s.h):
        e.activation(out=rbcg[:], in_=rbcg[:], func=AF.Ln, bias=float(EPS), scale=1.0)
        e.drain()
        e.activation(out=rbcg[:], in_=rbcg[:], func=AF.Exp, bias=0.0, scale=-0.5).then_inc(h, 1)
    b.emit("scalar", grstd)
    b.wait("vector", s_s, sst4)
    b.wait("vector", s_ld, s_ld.n)
    vnrm = {}
    for fi in range(NF_G):
        vv = s_v.inc()
        def nrm(e, fi=fi, h=s_v.h):
            e.drain()
            e.scalar_tensor_tensor(out=gate_t[fi][:], in0=gate_t[fi][:], scalar=nwt[:, fi:fi + 1],
                                   in1=rbcg[:], op0=ALU.mult, op1=ALU.mult)
            e.drain().then_inc(h, 1)
        b.emit("vector", nrm)
        vnrm[fi] = vv
    s_wo = [Sem(nc, es, "wo0"), Sem(nc, es, "wo1")]
    s_x2 = Sem(nc, es, "x2")
    def wo_src(dtile, half):
        return w_out[half * 2048:(half + 1) * 2048, dtile * 128:(dtile + 1) * 128].rearrange("(k p) f -> p k f", p=128)
    b.dma("gpsimd", s_wo[0], panel_dst(0), wo_src(0, 0))
    b.dma("gpsimd", s_wo[1], panel_dst(1), wo_src(0, 1))
    pcnt4 = {0: 16, 1: 16}
    tres = {}
    vres = {}
    for dtile in range(ND):
        if dtile == 0:
            b.wait("sync", s_t, s_t.n)
        b.dma("sync", s_x2, resid[dtile], x_in[dtile * 128:(dtile + 1) * 128, 3:1027])
        r4 = dtile % 2
        pA, pB = pb[4 + 2 * r4], pb[5 + 2 * r4]
        for half in range(2):
            n4 = 2 * dtile + half
            j = n4 % 2
            b.wait("tensor", s_wo[j], pcnt4[n4 % 2] if False else pcnt4[j])
            if half == 0 and dtile == 0:
                b.wait("tensor", s_v, vnrm[NF_G - 1])
            if dtile >= 2 and half == 0:
                b.wait("tensor", s_v, vres[dtile - 2])
            ti = s_t.inc()
            for k in range(16):
                kk = half * 16 + k
                st = (half == 0 and k == 0); sp = (half == 1 and k == 15)
                b.emit("tensor", lambda e, j=j, k=k, kk=kk, st=st, sp=sp, pA=pA, pB=pB, last=(k == 15), h=s_t.h: (
                    (e.matmul(pA[:], lhsT=wpan[j][:, k * 128:(k + 1) * 128], rhs=gate_t[kk][:, 0:512], start=st, stop=sp),
                     e.matmul(pB[:], lhsT=wpan[j][:, k * 128:(k + 1) * 128], rhs=gate_t[kk][:, 512:1024], start=st, stop=sp).then_inc(h, 1))[-1] if last else
                    (e.matmul(pA[:], lhsT=wpan[j][:, k * 128:(k + 1) * 128], rhs=gate_t[kk][:, 0:512], start=st, stop=sp),
                     e.matmul(pB[:], lhsT=wpan[j][:, k * 128:(k + 1) * 128], rhs=gate_t[kk][:, 512:1024], start=st, stop=sp))[-1]))
            tres[n4] = ti
            nxt = n4 + 2
            if nxt < 2 * ND:
                b.wait("gpsimd", s_t, ti)
                b.dma("gpsimd", s_wo[j], panel_dst(j), wo_src(nxt // 2, nxt % 2))
                pcnt4[j] = s_wo[j].n
        b.wait("vector", s_t, tres[2 * dtile + 1])
        b.wait("vector", s_x2, s_x2.n)
        vv = s_v.inc()
        def racc(e, dtile=dtile, pA=pA, pB=pB, h=s_v.h):
            e.drain()
            e.tensor_add(out=resid[dtile][:, 0:512], in0=resid[dtile][:, 0:512], in1=pA[:])
            e.tensor_add(out=resid[dtile][:, 512:1024], in0=resid[dtile][:, 512:1024], in1=pB[:]).then_inc(h, 1)
        b.emit("vector", racc)
        vres[dtile] = vv
        b.wait("sync", s_v, vv)
        b.dma("sync", s_st, x1_dram[dtile * 128:(dtile + 1) * 128, :], resid[dtile])
        for _ in range(4):
            emit_w2_step()
    while w2_gi[0] < NW2:
        emit_w2_step()
    b.wait("sync", s_st, s_st.n)
    b.barrier()

    if "x1" in debug:
        dbx1 = b.dbg("x1dbg", [D, T])
        for i2 in range(ND):
            b.dma("sync", s_st, dbx1[i2 * 128:(i2 + 1) * 128, :], resid[i2])
        b.wait("sync", s_st, s_st.n)

    x2bf = [regE[:, j * 1024:(j + 1) * 1024] for j in range(2)]
    mbc2 = regEf[:, 1024:2048]
    rbc2 = regEf[:, 2048:3072]
    amr = regEf[:, 3072:4096]
    ln2t = regEf[:, 4096:5120]
    scalebc = regCf[:, 0:1024]
    rinvbc = regCf[:, 1024:2048]
    rmul = regCf[:, 2048:3072]
    xq_t = [regAB[:, 32768 + i2 * 1024: 32768 + (i2 + 1) * 1024] for i2 in range(ND)]
    tdone2 = {}
    for i2 in range(ND):
        j = i2 % 2
        if i2 >= 2:
            b.wait("vector", s_t, tdone2[i2 - 2])
        v1 = s_v.inc()
        b.emit("vector", lambda e, i2=i2, j=j, h=s_v.h: e.tensor_copy(out=x2bf[j][:], in_=resid[i2][:]).then_inc(h, 1))
        b.wait("tensor", s_v, v1)
        t1 = s_t.inc()
        for ci, (c0, cw) in enumerate(TCO):
            b.emit("tensor", lambda e, c0=c0, cw=cw, i2=i2, j=j, ci=ci, last=(ci == 1), h=s_t.h: (
                e.matmul(pb[ci][:], lhsT=ones_sb[:], rhs=x2bf[j][:, c0:c0 + cw], start=(i2 == 0), stop=(i2 == ND - 1)).then_inc(h, 1) if last else
                e.matmul(pb[ci][:], lhsT=ones_sb[:], rhs=x2bf[j][:, c0:c0 + cw], start=(i2 == 0), stop=(i2 == ND - 1))))
        b.wait("vector", s_t, t1)
        v2 = s_v.inc()
        b.emit("vector", lambda e, i2=i2, j=j, h=s_v.h: e.tensor_mul(out=x2bf[j][:], in0=resid[i2][:], in1=resid[i2][:]).then_inc(h, 1))
        b.wait("tensor", s_v, v2)
        t2 = s_t.inc()
        for ci, (c0, cw) in enumerate(TCO):
            b.emit("tensor", lambda e, c0=c0, cw=cw, i2=i2, j=j, ci=ci, last=(ci == 1), h=s_t.h: (
                e.matmul(pb[2 + ci][:], lhsT=ones_sb[:], rhs=x2bf[j][:, c0:c0 + cw], start=(i2 == 0), stop=(i2 == ND - 1)).then_inc(h, 1) if last else
                e.matmul(pb[2 + ci][:], lhsT=ones_sb[:], rhs=x2bf[j][:, c0:c0 + cw], start=(i2 == 0), stop=(i2 == ND - 1))))
        tdone2[i2] = t2
    b.wait("vector", s_t, s_t.n)
    vst5 = s_v.inc()
    def fin2(e, h=s_v.h):
        e.tensor_scalar(out=mbc2[:, 0:512], in0=pb[0][:], scalar1=inv_d, scalar2=None, op0=ALU.mult)
        e.tensor_scalar(out=mbc2[:, 512:1024], in0=pb[1][:], scalar1=inv_d, scalar2=None, op0=ALU.mult)
        e.tensor_scalar(out=rbc2[:, 0:512], in0=pb[2][:], scalar1=inv_d, scalar2=None, op0=ALU.mult)
        e.tensor_scalar(out=rbc2[:, 512:1024], in0=pb[3][:], scalar1=inv_d, scalar2=None, op0=ALU.mult)
        e.drain()
        e.tensor_mul(out=ln2t[:], in0=mbc2[:], in1=mbc2[:])
        e.drain()
        e.tensor_sub(out=rbc2[:], in0=rbc2[:], in1=ln2t[:]).then_inc(h, 1)
    b.emit("vector", fin2)
    b.wait("scalar", s_v, vst5)
    sst5 = s_s.inc()
    def rstd2(e, h=s_s.h):
        e.activation(out=rbc2[:], in_=rbc2[:], func=AF.Ln, bias=float(EPS), scale=1.0)
        e.drain()
        e.activation(out=rbc2[:], in_=rbc2[:], func=AF.Exp, bias=0.0, scale=-0.5).then_inc(h, 1)
    b.emit("scalar", rstd2)
    b.wait("vector", s_s, sst5)
    vam0 = s_v.inc()
    def am0(e, h=s_v.h):
        e.memset(amr[:], 0.0).then_inc(h, 1)
    b.emit("vector", am0)
    sam = {}
    for i2 in range(ND):
        vv = s_v.inc()
        def lnt(e, i2=i2, h=s_v.h):
            e.drain()
            e.tensor_sub(out=ln2t[:], in0=resid[i2][:], in1=mbc2[:])
            e.drain()
            e.tensor_mul(out=ln2t[:], in0=ln2t[:], in1=rbc2[:]).then_inc(h, 1)
        b.emit("vector", lnt)
        b.wait("scalar", s_v, vv)
        sv = s_s.inc()
        b.emit("scalar", lambda e, h=s_s.h: e.activation(out=regEf[:, 0:1024], in_=ln2t[:], func=AF.Abs).then_inc(h, 1))
        sam[i2] = sv
        b.wait("vector", s_s, sv)
        vv2 = s_v.inc()
        def amx(e, h=s_v.h):
            e.drain()
            e.tensor_max(out=amr[:], in0=amr[:], in1=regEf[:, 0:1024]).then_inc(h, 1)
        b.emit("vector", amx)
    amax_bcast(amr, scalebc)
    vrm = s_v.inc()
    def mkrm(e, h=s_v.h):
        e.tensor_scalar(out=scalebc[:], in0=scalebc[:], scalar1=float(EPS), scalar2=None, op0=ALU.max)
        e.drain()
        e.reciprocal(out=rmul[:], in_=scalebc[:])
        e.drain()
        e.tensor_scalar(out=scalebc[:], in0=rmul[:], scalar1=128.0, scalar2=None, op0=ALU.mult)
        e.drain()
        e.reciprocal(out=rinvbc[:], in_=scalebc[:])
        e.drain()
        e.tensor_mul(out=rmul[:], in0=rbc2[:], in1=scalebc[:]).then_inc(h, 1)
    b.emit("vector", mkrm)
    vq1 = {}
    for i2 in range(ND):
        vv = s_v.inc()
        def q1(e, i2=i2, h=s_v.h):
            e.drain()
            e.tensor_sub(out=ln2t[:], in0=resid[i2][:], in1=mbc2[:])
            e.drain()
            e.tensor_mul(out=ln2t[:], in0=ln2t[:], in1=rmul[:])
            e.drain()
            e.tensor_scalar(out=ln2t[:], in0=ln2t[:], scalar1=RC, scalar2=RC, op0=ALU.add, op1=ALU.subtract)
            e.drain()
            e.tensor_scalar(out=xq_t[i2][:], in0=ln2t[:], scalar1=127.0, scalar2=-128.0, op0=ALU.min, op1=ALU.max).then_inc(h, 1)
        b.emit("vector", q1)
        vq1[i2] = vv
    b.barrier()

    if "xq" in debug:
        dbxq = b.dbg("xqdbg", [D, T])
        for i2 in range(ND):
            b.dma("gpsimd", s_gst, dbxq[i2 * 128:(i2 + 1) * 128, :], xq_t[i2][:])
        b.wait("gpsimd", s_gst, s_gst.n)

    b.dma("sync", s_ld, H128_sb[:], H128_c.ap())
    b.dma("sync", s_ld, H64b_sb[:], H64b_c.ap())
    b.wait("sync", s_ld, s_ld.n)
    b.wait("gpsimd", s_cg, s_cg.n)
    b.barrier()
    sbcg = regEf[:, 0:1024]
    gv2 = [regEf[:, 1024:2048], regEf[:, 3072:4096]]
    intert2 = [regE[:, 4096:5120], regE[:, 8192:9216]]
    ztile2 = [regE[:, 5120:6144], regE[:, 9216:10240]]
    s_zs = [Sem(nc, es, "zs0"), Sem(nc, es, "zs1")]
    ssil6 = {}
    tz66 = {}
    vpre = s_v.inc()
    def mksbcg(e, h=s_v.h):
        e.tensor_scalar(out=sbcg[:], in0=rinvbc[:], scalar1=wqrow[:, 4:5], scalar2=None, op0=ALU.mult).then_inc(h, 1)
    b.emit("vector", mksbcg)
    s_wp6g = [Sem(nc, es, "wp6g0"), Sem(nc, es, "wp6g1")]
    s_wp6u = [Sem(nc, es, "wp6u0"), Sem(nc, es, "wp6u1")]
    gpan6 = [wpan[0][:], regC[:, 4096:6144]]
    upan6 = [wpan[1][:], regC[:, 6144:8192]]
    def pan6_dst(buf):
        return buf.rearrange("p (k f) -> p k f", f=128)
    def load_gu_panel(sem, buf, wout, ot):
        r = ot // 8
        c0 = (ot % 8) * 128
        dst = pan6_dst(buf)
        for p in range(4):
            src = wout.ap()[p * 4096 + r * 512:p * 4096 + (r + 1) * 512, c0:c0 + 128].rearrange("(k q) f -> q k f", q=128)
            b.dma("sync", sem, dst[:, 4 * p:4 * p + 4, :], src)
    load_gu_panel(s_wp6g[0], gpan6[0], wqg_out, 0)
    load_gu_panel(s_wp6u[0], upan6[0], wqu_out, 0)
    NOT = FFI // 128
    tg6 = {}
    vz6 = {}
    sz6 = {}
    for ot in range(NOT):
        pi = ot % 2
        pgA, pgB = pb[0 + 2 * pi], pb[1 + 2 * pi]
        puA, puB = pb[4 + 2 * pi], pb[5 + 2 * pi]
        if ot + 1 < NOT:
            if ot >= 1:
                b.wait("sync", s_t, tg6[ot - 1])
            jn = (ot + 1) % 2
            load_gu_panel(s_wp6g[jn], gpan6[jn], wqg_out, ot + 1)
            load_gu_panel(s_wp6u[jn], upan6[jn], wqu_out, ot + 1)
        b.wait("tensor", s_wp6g[pi], s_wp6g[pi].n - (64 if (ot + 1 < NOT and (ot + 1) % 2 == pi) else 0))
        b.wait("tensor", s_wp6u[pi], s_wp6u[pi].n - (64 if (ot + 1 < NOT and (ot + 1) % 2 == pi) else 0))
        if ot == 0:
            b.wait("tensor", s_v, vpre)
        if ot >= 2:
            b.wait("tensor", s_s, sz6[ot - 2])
            b.wait("tensor", s_v, vz6[ot - 2])
        ti = s_t.inc()
        for k in range(ND):
            st = (k == 0); sp = (k == ND - 1)
            b.emit("tensor", lambda e, k=k, st=st, sp=sp, pgA=pgA, pgB=pgB, gp=gpan6[pi]: (
                e.matmul(pgA[:], lhsT=gp[:, k * 128:(k + 1) * 128], rhs=xq_t[k][:, 0:512], start=st, stop=sp),
                e.matmul(pgB[:], lhsT=gp[:, k * 128:(k + 1) * 128], rhs=xq_t[k][:, 512:1024], start=st, stop=sp))[-1] and None)
        for k in range(ND):
            st = (k == 0); sp = (k == ND - 1)
            b.emit("tensor", lambda e, k=k, st=st, sp=sp, puA=puA, puB=puB, up=upan6[pi], last=(k == ND - 1), h=s_t.h: (
                (e.matmul(puA[:], lhsT=up[:, k * 128:(k + 1) * 128], rhs=xq_t[k][:, 0:512], start=st, stop=sp),
                 e.matmul(puB[:], lhsT=up[:, k * 128:(k + 1) * 128], rhs=xq_t[k][:, 512:1024], start=st, stop=sp).then_inc(h, 1))[-1] if last else
                (e.matmul(puA[:], lhsT=up[:, k * 128:(k + 1) * 128], rhs=xq_t[k][:, 0:512], start=st, stop=sp),
                 e.matmul(puB[:], lhsT=up[:, k * 128:(k + 1) * 128], rhs=xq_t[k][:, 512:1024], start=st, stop=sp))[-1] and None))
        tg6[ot] = ti
        j6 = ot % 2
        b.wait("vector", s_t, ti)
        if ot >= 2:
            b.wait("vector", s_s, ssil6[ot - 2])
        vv = s_v.inc()
        def ep6(e, j6=j6, pgA=pgA, pgB=pgB, h=s_v.h):
            e.drain()
            e.tensor_mul(out=gv2[j6][:, 0:512], in0=pgA[:], in1=sbcg[:, 0:512])
            e.tensor_mul(out=gv2[j6][:, 512:1024], in0=pgB[:], in1=sbcg[:, 512:1024]).then_inc(h, 1)
        b.emit("vector", ep6)
        b.wait("scalar", s_v, vv)
        sv = s_s.inc()
        b.emit("scalar", lambda e, j6=j6, h=s_s.h: e.activation(out=gv2[j6][:], in_=gv2[j6][:], func=AF.Silu).then_inc(h, 1))
        ssil6[ot] = sv
        b.wait("vector", s_s, sv)
        if ot >= 2:
            b.wait("vector", s_t, tz66[ot - 2])
        vv2 = s_v.inc()
        def ep6b(e, j6=j6, puA=puA, puB=puB, h=s_v.h):
            e.drain()
            e.tensor_mul(out=intert2[j6][:, 0:512], in0=gv2[j6][:, 0:512], in1=puA[:])
            e.tensor_mul(out=intert2[j6][:, 512:1024], in0=gv2[j6][:, 512:1024], in1=puB[:]).then_inc(h, 1)
        b.emit("vector", ep6b)
        vz6[ot] = vv2
        b.wait("tensor", s_v, vv2)
        tz = s_t.inc()
        def st1(e, j6=j6, pgA=pgA, pgB=pgB, h=s_t.h):
            e.matmul(pgA[:], lhsT=H128_sb[:], rhs=intert2[j6][:, 0:512], start=True, stop=True)
            e.matmul(pgB[:], lhsT=H128_sb[:], rhs=intert2[j6][:, 512:1024], start=True, stop=True).then_inc(h, 1)
        b.emit("tensor", st1)
        tz66[ot] = tz
        b.wait("scalar", s_t, tz)
        b.wait("scalar", s_zs[j6], s_zs[j6].n)
        sz = s_s.inc()
        def ze(e, j6=j6, pgA=pgA, pgB=pgB, h=s_s.h):
            e.copy(out=ztile2[j6][:, 0:512], in_=pgA[:])
            e.copy(out=ztile2[j6][:, 512:1024], in_=pgB[:]).then_inc(h, 1)
        b.emit("scalar", ze)
        sz6[ot] = sz
        b.wait("sync", s_s, sz)
        b.dma("sync", s_zs[j6], zdram[ot * 128:(ot + 1) * 128, :], ztile2[j6][:])
    b.wait("sync", s_zs[0], s_zs[0].n)
    b.wait("sync", s_zs[1], s_zs[1].n)
    b.barrier()

    vtile = [regE[:, 0:1024], regE[:, 1024:2048], regC[:, 8192:9216], regC[:, 9216:10240],
             regC[:, 10240:11264], regC[:, 11264:12288]]
    abstl = regEf[:, 1024:2048]
    amr2 = regEf[:, 2048:3072]
    qmulbc = regEf[:, 3072:4096]
    rfinbc = regEf[:, 4096:5120]
    W_t = [regAB[:, m * 1024:(m + 1) * 1024] for m in range(64)]
    s_v6 = [Sem(nc, es, "v6_%d" % j) for j in range(6)]
    zdr = zdram.rearrange("(i j) t -> i j t", j=128)
    tW = {}
    sW = {}
    vA = {}
    for m in range(64):
        j = m % 6
        pi = m % 4
        pwA, pwB = pb[2 * pi], pb[2 * pi + 1]
        if m >= 6:
            b.wait("sync", s_t, tW[m - 6])
        for jr in range(2):
            for hh in range(2):
                b.dma("sync", s_v6[j], vtile[j][jr * 64 + hh * 32:jr * 64 + (hh + 1) * 32, :],
                      zdr[hh * 32:(hh + 1) * 32, 2 * m + jr, :])
        b.wait("tensor", s_v6[j], s_v6[j].n)
        if m >= 4:
            b.wait("tensor", s_v, vA[m - 4])
        ti = s_t.inc()
        def st2f(e, j=j, pwA=pwA, pwB=pwB, h=s_t.h):
            e.matmul(pwA[:], lhsT=H64b_sb[:], rhs=vtile[j][:, 0:512], start=True, stop=True)
            e.matmul(pwB[:], lhsT=H64b_sb[:], rhs=vtile[j][:, 512:1024], start=True, stop=True).then_inc(h, 1)
        b.emit("tensor", st2f)
        tW[m] = ti
        b.wait("scalar", s_t, ti)
        sv = s_s.inc()
        def evW(e, m=m, pwA=pwA, pwB=pwB, h=s_s.h):
            e.copy(out=W_t[m][:, 0:512], in_=pwA[:])
            e.copy(out=W_t[m][:, 512:1024], in_=pwB[:]).then_inc(h, 1)
        b.emit("scalar", evW)
        sW[m] = sv
        b.wait("vector", s_s, sv)
        vv = s_v.inc()
        def mmx(e, m=m, pwA=pwA, pwB=pwB, h=s_v.h):
            e.drain()
            if m == 0:
                e.tensor_copy(out=amr2[:, 0:512], in_=pwA[:])
                e.tensor_copy(out=amr2[:, 512:1024], in_=pwB[:])
                e.tensor_copy(out=abstl[:, 0:512], in_=pwA[:])
                e.tensor_copy(out=abstl[:, 512:1024], in_=pwB[:]).then_inc(h, 1)
            else:
                e.tensor_max(out=amr2[:, 0:512], in0=amr2[:, 0:512], in1=pwA[:])
                e.tensor_max(out=amr2[:, 512:1024], in0=amr2[:, 512:1024], in1=pwB[:])
                e.tensor_tensor(out=abstl[:, 0:512], in0=abstl[:, 0:512], in1=pwA[:], op=ALU.min)
                e.tensor_tensor(out=abstl[:, 512:1024], in0=abstl[:, 512:1024], in1=pwB[:], op=ALU.min).then_inc(h, 1)
        b.emit("vector", mmx)
        vA[m] = vv
    SQN = float(np.sqrt(FFI))
    vcmb = s_v.inc()
    def cmb6(e, h=s_v.h):
        e.drain()
        e.tensor_scalar(out=abstl[:], in0=abstl[:], scalar1=-1.0, scalar2=None, op0=ALU.mult)
        e.drain()
        e.tensor_max(out=amr2[:], in0=amr2[:], in1=abstl[:]).then_inc(h, 1)
    b.emit("vector", cmb6)
    amax_bcast(amr2, abstl)
    vtr2 = s_v.inc()
    def rows6(e, h=s_v.h):
        e.tensor_mul(out=abstl[:], in0=abstl[:], in1=rinvbc[:])
        e.drain()
        e.tensor_scalar(out=abstl[:], in0=abstl[:], scalar1=wqrow[:, 5:6], scalar2=None, op0=ALU.mult)
        e.drain()
        e.tensor_scalar(out=abstl[:], in0=abstl[:], scalar1=float(1.0 / SQN), scalar2=float(EPS), op0=ALU.mult, op1=ALU.max)
        e.drain()
        e.reciprocal(out=amr2[:], in_=abstl[:])
        e.drain()
        e.tensor_scalar(out=amr2[:], in0=amr2[:], scalar1=128.0, scalar2=None, op0=ALU.mult)
        e.drain()
        e.tensor_mul(out=qmulbc[:], in0=amr2[:], in1=rinvbc[:])
        e.drain()
        e.tensor_scalar(out=qmulbc[:], in0=qmulbc[:], scalar1=wqrow[:, 5:6], scalar2=float(1.0 / SQN), op0=ALU.mult, op1=ALU.mult)
        e.drain()
        e.reciprocal(out=rfinbc[:], in_=amr2[:])
        e.drain()
        e.tensor_scalar(out=rfinbc[:], in0=rfinbc[:], scalar1=wqrow[:, 6:7], scalar2=None, op0=ALU.mult).then_inc(h, 1)
    b.emit("vector", rows6)
    b.wait("vector", s_v, vtr2)
    b.wait("gpsimd", s_v, vtr2)
    vq2 = {}
    for m in range(64):
        scr = abstl if m % 2 == 0 else amr2
        eng = "vector"
        sem = s_v
        vv = sem.inc()
        def q2(e, m=m, scr=scr, h=sem.h):
            e.drain()
            e.tensor_mul(out=scr[:], in0=W_t[m][:], in1=qmulbc[:])
            e.drain()
            e.tensor_scalar(out=scr[:], in0=scr[:], scalar1=RC, scalar2=RC, op0=ALU.add, op1=ALU.subtract)
            e.drain()
            e.tensor_scalar(out=W_t[m][:], in0=scr[:], scalar1=127.0, scalar2=-128.0, op0=ALU.min, op1=ALU.max).then_inc(h, 1)
        b.emit(eng, q2)
        vq2[m] = (eng, vv)
    b.barrier()

    dpan = [regC[:, 4096 + j * 4096: 8192 + j * 4096] for j in range(2)]
    outt = [regCf[:, 6144 + j * 1024: 7168 + j * 1024] for j in range(2)]
    x1t = [regEf[:, j * 1024:(j + 1) * 1024] for j in range(2)]
    s_dp = [Sem(nc, es, "dp0"), Sem(nc, es, "dp1")]
    s_x6 = Sem(nc, es, "x6")
    s_yo = [Sem(nc, es, "yo0"), Sem(nc, es, "yo1")]
    wdr = wqd_out.ap()
    def dpan_src(ot2, half):
        r = ot2 // 2
        c0 = (ot2 % 2) * 128
        blk = wdr[r * FFI:(r + 1) * FFI, c0:c0 + 128]
        return blk.rearrange("(i j) o -> i j o", j=128)[:, 64 * half:64 * half + 64, :].transpose([1, 0, 2]).rearrange("j i o -> ???") if False else blk
    def dpan_src2(ot2, half):
        r = ot2 // 2
        c0 = (ot2 % 2) * 128
        blk = wdr[r * FFI:(r + 1) * FFI, c0:c0 + 128]
        v = blk.rearrange("(i m b) o -> b i m o", b=2, m=64)
        return v[:, :, 32 * half:32 * half + 32, :]
    def dpan_dst(j):
        return wpanD[j].rearrange("p (m o) -> p m o", o=128)
    wpanD = dpan
    tdn = {}
    vdn = {}
    def dpan_load(j, ot2, half):
        v = dpan_src2(ot2, half)
        dst = wpanD[j].rearrange("p (m o) -> p m o", o=128)
        b.dma("sync", s_dp[j], dst[0:64, :, :], v[0, :, :, :])
        b.dma("sync", s_dp[j], dst[64:128, :, :], v[1, :, :, :])
    dpan_load(0, 0, 0)
    dpan_load(1, 0, 1)
    pcntd = {0: 32, 1: 32}
    for ot2 in range(ND):
        pi = ot2 % 2
        pdA, pdB = pb[4 + 2 * pi], pb[5 + 2 * pi]
        b.dma("sync", s_x6, x1t[pi][:], x1_dram[ot2 * 128:(ot2 + 1) * 128, :])
        for half in range(2):
            n6 = 2 * ot2 + half
            j = n6 % 2
            b.wait("tensor", s_dp[j], pcntd[j])
            if ot2 >= 2 and half == 0:
                b.wait("tensor", s_v, vdn[ot2 - 2])
            ti = s_t.inc()
            for mi in range(32):
                m = half * 32 + mi
                st = (half == 0 and mi == 0); sp = (half == 1 and mi == 31)
                b.emit("tensor", lambda e, j=j, mi=mi, m=m, st=st, sp=sp, pdA=pdA, pdB=pdB, last=(mi == 31), h=s_t.h: (
                    (e.matmul(pdA[:], lhsT=wpanD[j][:, mi * 128:(mi + 1) * 128], rhs=W_t[m][:, 0:512], start=st, stop=sp),
                     e.matmul(pdB[:], lhsT=wpanD[j][:, mi * 128:(mi + 1) * 128], rhs=W_t[m][:, 512:1024], start=st, stop=sp).then_inc(h, 1))[-1] if last else
                    (e.matmul(pdA[:], lhsT=wpanD[j][:, mi * 128:(mi + 1) * 128], rhs=W_t[m][:, 0:512], start=st, stop=sp),
                     e.matmul(pdB[:], lhsT=wpanD[j][:, mi * 128:(mi + 1) * 128], rhs=W_t[m][:, 512:1024], start=st, stop=sp))[-1] and None))
            tdn[n6] = ti
            nxt = n6 + 2
            if nxt < 2 * ND:
                b.wait("sync", s_t, ti)
                dpan_load(j, nxt // 2, nxt % 2)
                pcntd[j] = s_dp[j].n
        b.wait("vector", s_t, tdn[2 * ot2 + 1])
        b.wait("vector", s_x6, s_x6.n)
        if ot2 >= 2:
            b.wait("vector", s_yo[pi], s_yo[pi].n)
        vv = s_v.inc()
        def dne(e, ot2=ot2, pi=pi, pdA=pdA, pdB=pdB, h=s_v.h):
            e.tensor_mul(out=outt[pi][:, 0:512], in0=pdA[:], in1=rfinbc[:, 0:512])
            e.tensor_mul(out=outt[pi][:, 512:1024], in0=pdB[:], in1=rfinbc[:, 512:1024])
            e.drain()
            e.tensor_add(out=outt[pi][:], in0=outt[pi][:], in1=x1t[pi][:]).then_inc(h, 1)
        b.emit("vector", dne)
        vdn[ot2] = vv
        b.wait("sync", s_v, vv)
        b.dma("sync", s_yo[pi], y_out[ot2 * 128:(ot2 + 1) * 128, :], outt[pi][:])
    b.wait("sync", s_yo[0], s_yo[0].n)
    b.wait("sync", s_yo[1], s_yo[1].n)
    b.wait("tensor", s_yo[0], s_yo[0].n)
    b.wait("tensor", s_yo[1], s_yo[1].n)

    if "xc" in debug:
        dbgx = b.dbg("xc", [INTER, T])
        for c in range(NCH):
            for i in range(NF_G):
                b.dma("gpsimd", s_gst, dbgx[i * 128:(i + 1) * 128, c * 128:(c + 1) * 128], xpr[:, c, i, :])
        dbgB = b.dbg("Bdbg", [128, T]); dbgC = b.dbg("Cdbg", [128, T]); dbgdt = b.dbg("dtdbg", [64, TH])
        b.dma("gpsimd", s_gst, dbgB, Bc[:])
        b.dma("gpsimd", s_gst, dbgC, Cc[:])
        b.dma("gpsimd", s_gst, dbgdt, dt32[:])
        b.wait("gpsimd", s_gst, s_gst.n)

    return b


def _stage(inputs):
    x = inputs["L0_attn_hidden_states"]
    cw = np.ascontiguousarray(inputs["L0_attn_conv_w"].reshape(NF_XBC, 128, 4).transpose(1, 0, 2).reshape(128, NF_XBC * 4))
    cb = np.ascontiguousarray(inputs["L0_attn_conv_b"].reshape(NF_XBC, 128).T)
    nw = np.ascontiguousarray(inputs["L0_attn_norm_w"].reshape(NF_G, 128).T)
    in_maps = []
    for c in range(NCORES):
        bi, hf = c // 2, c % 2
        t0 = hf * T
        xs = np.zeros((TH, D), np.float32)
        xs[3:] = x[bi, t0:t0 + T]
        if hf == 1:
            xs[:3] = x[bi, t0 - 3:t0]
        m = {
            "x_in": np.ascontiguousarray(xs.T),
            "w_xbc": inputs["L0_attn_W_xBC"],
            "w_z": inputs["L0_attn_W_z"],
            "w_dt": inputs["L0_attn_W_dt"],
            "w_out": inputs["L0_attn_W_out"],
            "cw_in": cw, "cb_in": cb, "nw_in": nw,
            "dtb_in": inputs["L0_attn_dt_bias"].reshape(64, 1),
            "alog_in": inputs["L0_attn_A_log"].reshape(64, 1),
            "dvec_in": inputs["L0_attn_D"].reshape(64, 1),
            "wg_sh": np.ascontiguousarray(inputs["L0_ffn_w_gate"][c * T:(c + 1) * T].T),
            "wu_sh": np.ascontiguousarray(inputs["L0_ffn_w_up"][c * T:(c + 1) * T].T),
            "wd_sh": np.ascontiguousarray(inputs["L0_ffn_w_down"][c * 256:(c + 1) * 256].T),
            "smask": np.full((128, 1), float(hf), np.float32),
        }
        in_maps.append(m)
    return in_maps


def kernel(**inputs):
    in_maps = _stage(inputs)
    b = build()
    nc = b.finish()
    res = run_bass_kernel_spmd(nc, in_maps, core_ids=list(range(NCORES)))
    out = np.zeros((4, 2048, 2048), np.float32)
    for c in range(NCORES):
        bi, hf = c // 2, c % 2
        out[bi, hf * T:(hf + 1) * T, :] = res.results[c]["y_out"].T
    return out
